# version 08
# baseline (speedup 1.0000x reference)
import dataclasses
import os

import numpy as np
import ml_dtypes

import concourse.bass as bass
import concourse.mybir as mybir
from concourse.bass_utils import run_bass_kernel_spmd

F32 = mybir.dt.float32
BF16 = mybir.dt.bfloat16
I32 = mybir.dt.int32
I8 = mybir.dt.int8
AF = mybir.ActivationFunctionType
ALU = mybir.AluOpType
AX = mybir.AxisListType.X

NCORE = 8
B, T, D = 2, 4096, 2048
NT = 1024
H, DK, DV = 8, 256, 256
HID = 5632
NHC = HID // 128
NDC = D // 128
CH = 128
NCH = T // CH
GT = B * T
NW = 16
NG = 16
LN_EPS = 1e-5
L2_EPS = 1e-6
RMS_EPS = 1e-5

MK_DBG = os.environ.get("MK_DBG", "")
MK_PH = os.environ.get("MK_PH", "full")


class Ctr:
    def __init__(self, nc, name):
        self.h = nc.alloc_semaphore(name)
        self.v = 0


class Arena:
    def __init__(self, nc, name, nbytes):
        self.t = nc.alloc_sbuf_tensor(name, [128, nbytes], I8)
        self.nbytes = nbytes
        self.off = 0

    def reset(self, off=0):
        self.off = off

    def take(self, shape, dtype):
        esz = mybir.dt.size(dtype)
        start = (self.off + 31) // 32 * 32
        n = shape[1] * esz
        assert shape[0] <= 128
        assert start + n <= self.nbytes, (start, n, self.nbytes)
        self.off = start + n
        return self.t[0:shape[0], start:start + n].bitcast(dtype)


def ts(i, n):
    return slice(i * n, (i + 1) * n)


def build_nc():
    nc = bass.Bass("TRN2", target_bir_lowering=False)

    x_in = nc.dram_tensor("x_loc", [NT, D], F32, kind="ExternalInput").ap()
    wg_t = nc.dram_tensor("wg_t", [NHC, 128, D], I32, kind="ExternalInput").ap()
    wu_t = nc.dram_tensor("wu_t", [NHC, 128, D], I32, kind="ExternalInput").ap()
    wd_t = nc.dram_tensor("wd_t", [4, NHC, 128, 512], I32, kind="ExternalInput").ap()
    sg_in = nc.dram_tensor("sg_in", [128, NHC], F32, kind="ExternalInput").ap()
    su_in = nc.dram_tensor("su_in", [128, NHC], F32, kind="ExternalInput").ap()
    sd_in = nc.dram_tensor("sd_in", [1, D], F32, kind="ExternalInput").ap()
    wqkv_in = nc.dram_tensor("wqkv_in", [NDC, 128, 768], F32, kind="ExternalInput").ap()
    wb_in = nc.dram_tensor("wb_in", [NDC, 128, 1], F32, kind="ExternalInput").ap()
    convw_in = nc.dram_tensor("convw_in", [2, 128, 12], F32, kind="ExternalInput").ap()
    wo_in = nc.dram_tensor("wo_in", [D, D], F32, kind="ExternalInput").ap()
    nrmw_in = nc.dram_tensor("nrmw_in", [128, NDC], F32, kind="ExternalInput").ap()
    c_ident = nc.dram_tensor("c_ident", [128, 128], BF16, kind="ExternalInput").ap()
    c_identf = nc.dram_tensor("c_identf", [128, 128], F32, kind="ExternalInput").ap()
    c_twoI = nc.dram_tensor("c_twoI", [128, 512], BF16, kind="ExternalInput").ap()
    c_slmask = nc.dram_tensor("c_slmask", [128, 128], BF16, kind="ExternalInput").ap()
    c_iumask = nc.dram_tensor("c_iumask", [128, 128], BF16, kind="ExternalInput").ap()
    c_ones1 = nc.dram_tensor("c_ones1", [1, 128], BF16, kind="ExternalInput").ap()
    c_half1 = nc.dram_tensor("c_half1", [1, 128], BF16, kind="ExternalInput").ap()
    c_onesc = nc.dram_tensor("c_onesc", [128, 1], BF16, kind="ExternalInput").ap()
    c_ones1f = nc.dram_tensor("c_ones1f", [1, 128], F32, kind="ExternalInput").ap()
    ooff_in = nc.dram_tensor("ooff_in", [8, 1], I32, kind="ExternalInput").ap()
    fconst_in = nc.dram_tensor("fconst_in", [128, 8], F32, kind="ExternalInput").ap()

    out_d = nc.dram_tensor("out_loc", [NT, D], F32, kind="ExternalOutput").ap()
    dbg_d = nc.dram_tensor("dbg", [NT, D], F32, kind="ExternalOutput").ap() \
        if MK_DBG else None

    y0b = nc.dram_tensor("y0b", [NT, D], BF16).ap()
    x1b = nc.dram_tensor("x1b", [NT, D], F32).ap()
    y1c = nc.dram_tensor("y1c", [NT, D], BF16).ap()
    gy0 = nc.dram_tensor("gy0", [NCORE * 512, D], BF16).ap()
    gy1 = nc.dram_tensor("gy1", [NCORE * 512, D], BF16).ap()
    q_s = nc.dram_tensor("q_s", [DK, GT], BF16).ap()
    k_s = nc.dram_tensor("k_s", [DK, GT], BF16).ap()
    v_s = nc.dram_tensor("v_s", [DV, GT], BF16).ap()
    o_c = nc.dram_tensor("o_c", [GT, DV], BF16).ap()
    go = nc.dram_tensor("go", [NCORE * GT, DV], BF16).ap()
    o_m = nc.dram_tensor("o_m", [H * NT, DV], BF16).ap()
    x2b = nc.dram_tensor("x2b", [NT, D], F32).ap()
    y2b = nc.dram_tensor("y2b", [NT, D], BF16).ap()
    wg_bf = nc.dram_tensor("wg_bf", [NHC, 128, D], BF16).ap()
    wu_bf = nc.dram_tensor("wu_bf", [NHC, 128, D], BF16).ap()
    wd_bf = nc.dram_tensor("wd_bf", [4, NHC, 128, 512], BF16).ap()
    beta_d = nc.dram_tensor("beta_d", [1, GT], F32).ap()

    rg = [list(range(NCORE))]

    tarena = nc.alloc_sbuf_tensor("tarena", [128, NDC * 1024], BF16,
                                  is_transpose_dest=True)
    hid = nc.alloc_sbuf_tensor("hid", [128, NHC * 1024], BF16)
    ar = Arena(nc, "arena", 64 * 1024)
    sgu_sb = nc.alloc_sbuf_tensor("sgu_sb", [128, 2 * NHC], F32)
    sdb_sb = nc.alloc_sbuf_tensor("sdb_sb", [128, D], F32)
    consts = nc.alloc_sbuf_tensor("consts", [128, 1024], BF16)
    identf_sb = nc.alloc_sbuf_tensor("identf_sb", [128, 128], F32)
    ones_sb = nc.alloc_sbuf_tensor("ones_sb", [1, 256], BF16)
    convw_sb = nc.alloc_sbuf_tensor("convw_sb", [128, 24], F32)
    cdiag = nc.alloc_sbuf_tensor("cdiag", [128, 24 * 128], BF16)
    wb_sb = nc.alloc_sbuf_tensor("wb_sb", [128, NDC], BF16)
    nrmw_sb = nc.alloc_sbuf_tensor("nrmw_sb", [128, NDC], F32)
    ooff_sb = nc.alloc_sbuf_tensor("ooff_sb", [8, 1], I32)
    onesc_sb = nc.alloc_sbuf_tensor("onesc_sb", [128, 1], BF16)
    ones1f_sb = nc.alloc_sbuf_tensor("ones1f_sb", [1, 128], F32)
    fc_sb = nc.alloc_sbuf_tensor("fc_sb", [128, 8], F32)

    ident = consts[:, 0:128]
    twoI = consts[:, 128:640]
    slm = consts[:, 640:768]
    ium = consts[:, 768:896]
    ones1 = ones_sb[0:1, 0:128]
    half1 = ones_sb[0:1, 128:256]

    pb = [nc.alloc_psum_tensor(f"pb{i}", [128, 512], F32) for i in range(8)]

    cP = Ctr(nc, "cP")
    cV = Ctr(nc, "cV")
    cA = Ctr(nc, "cA")
    cO = Ctr(nc, "cO")
    wl = [Ctr(nc, f"wl{i}") for i in range(4)]
    xl = [Ctr(nc, f"xl{i}") for i in range(4)]
    sB = [Ctr(nc, f"sB{i}") for i in range(2)]
    sOne = Ctr(nc, "sOne")
    st_y0 = Ctr(nc, "st_y0")
    st_x1 = Ctr(nc, "st_x1")
    st_y1 = Ctr(nc, "st_y1")
    st_qkv = Ctr(nc, "st_qkv")
    st_o = Ctr(nc, "st_o")
    st_oh = Ctr(nc, "st_oh")
    st_y1h = Ctr(nc, "st_y1h")
    st_x2 = Ctr(nc, "st_x2")
    st_y2 = Ctr(nc, "st_y2")
    st_out = Ctr(nc, "st_out")
    svV = Ctr(nc, "svV")
    svA = Ctr(nc, "svA")
    svO = Ctr(nc, "svO")
    st_cgu = Ctr(nc, "st_cgu")
    st_cd = Ctr(nc, "st_cd")
    sC = Ctr(nc, "sC")

    def SY(e, c, ins):
        c.v += 1
        ins.then_inc(c.h, 1)
        e.wait_ge(c.h, c.v)

    def FIN(e, sv, c, ins, k=1):
        SY(e, sv, ins)
        c.v += k
        e.nop().then_inc(c.h, k)

    def FND(e, c, ins, k=1):
        c.v += k
        ins.then_inc(c.h, k)

    st_of = {"y0b": st_y0, "x1b": st_x1, "y1c": st_y1, "x2b": st_x2,
             "y2b": st_y2, "out": st_out}

    with nc.Block() as blk:
        ar.reset()
        sd_row = ar.take([1, D], F32)
        sd_bf = ar.take([1, D], BF16)
        b_one = sOne.v

        @blk.sync
        def _(e):
            for dst, src in [
                (consts[:, 0:128], c_ident), (consts[:, 128:640], c_twoI),
                (consts[:, 640:768], c_slmask), (consts[:, 768:896], c_iumask),
                (identf_sb[:, :], c_identf),
                (ones_sb[0:1, 0:128], c_ones1), (ones_sb[0:1, 128:256], c_half1),
                (sgu_sb[:, 0:NHC], sg_in), (sgu_sb[:, NHC:2 * NHC], su_in),
                (convw_sb[:, 0:12], convw_in[0]), (convw_sb[:, 12:24], convw_in[1]),
                (nrmw_sb[:, :], nrmw_in),
                (ooff_sb[:], ooff_in), (sd_row, sd_in), (fc_sb[:], fconst_in),
                (onesc_sb[:], c_onesc), (ones1f_sb[:], c_ones1f),
            ]:
                e.dma_start(dst, src).then_inc(sOne.h, 16)
                sOne.v += 16

        @blk.tensor
        def _(e):
            e.wait_ge(cV.h, cV.v + 1)
            for i in range(4):
                ins = e.matmul(pb[i][:, 0:512], half1, sd_bf[0:1, ts(i, 512)],
                               start=True, stop=True)
            ins.then_inc(cP.h, 1)
            cP.v += 1

        @blk.vector
        def _(e):
            e.wait_ge(sOne.h, b_one + 17 * 16)
            ins = e.tensor_copy(sd_bf, sd_row)
            FIN(e, svV, cV, ins)
            e.wait_ge(cP.h, cP.v)
            for i in range(4):
                ins = e.tensor_copy(sdb_sb[:, ts(i, 512)], pb[i][:, 0:512])
            FIN(e, svV, cV, ins)
            for m_ in range(3):
                for c_ in range(2):
                    for s_ in range(4):
                        j_ = (2 * m_ + c_) * 4 + s_
                        wc = convw_sb[:, 12 * c_ + 4 * m_ + (3 - s_):
                                      12 * c_ + 4 * m_ + (4 - s_)]
                        ins = e.tensor_scalar(cdiag[:, ts(j_, 128)], ident,
                                              wc, None, ALU.mult)
            FIN(e, svV, cV, ins)

        @blk.scalar
        def _(e):
            pass

        @blk.gpsimd
        def _(e):
            pass

    def emit_ln(x_src, y_dst, in_waits, st_out_c, half_ctr=None, prog=None):
        with nc.Block() as blk:
            ar.reset()
            xs = [ar.take([128, D], F32) for _ in range(2)]
            scr = ar.take([128, D], F32)
            stat = ar.take([128, 16], F32)
            yt = [ar.take([128, D], BF16) for _ in range(2)]
            b_v, b_c = cV.v, cA.v
            iw = [(c.h, c.v) for c in (in_waits or [])]
            marks = []

            @blk.scalar
            def _(e):
                for hh_, vv_ in iw:
                    e.wait_ge(hh_, vv_)
                for t in range(8):
                    if t == 0:
                        if prog is not None:
                            e.wait_ge(*prog[0])
                        e.dma_start(xs[0], x_src[ts(0, 128), :]).then_inc(
                            xl[0].h, 16)
                        xl[0].v += 16
                    if t + 1 < 8:
                        if t >= 1:
                            e.wait_ge(cV.h, b_v + 2 * (t - 1) + 2)
                        if prog is not None:
                            e.wait_ge(*prog[t + 1])
                        e.dma_start(xs[(t + 1) % 2],
                                    x_src[ts(t + 1, 128), :]).then_inc(
                                        xl[(t + 1) % 2].h, 16)
                        xl[(t + 1) % 2].v += 16
                    e.wait_ge(xl[t % 2].h, xl[t % 2].v)
                    if t >= 1:
                        e.wait_ge(cV.h, b_v + 2 * (t - 1) + 1)
                    ins = e.activation(scr, xs[t % 2], AF.Square,
                                       accum_out=stat[:, 1:2])
                    FIN(e, svA, cA, ins)
                    e.wait_ge(cV.h, b_v + 2 * t + 1)
                    ins = e.activation(stat[:, 6:7], stat[:, 5:6], AF.Sqrt,
                                       bias=fc_sb[:, 0:1])
                    FIN(e, svA, cA, ins)
                    e.wait_ge(cV.h, b_v + 2 * t + 2)
                    ctr = half_ctr if (half_ctr is not None and t < 4) \
                        else st_out_c
                    e.dma_start(y_dst[ts(t, 128), :], yt[t % 2]).then_inc(
                        ctr.h, 16)
                    ctr.v += 16
                    marks.append((ctr.h, ctr.v))

            @blk.vector
            def _(e):
                for t in range(8):
                    e.wait_ge(cA.h, b_c + 2 * t + 1)
                    e.reduce_sum(stat[:, 0:1], xs[t % 2], axis=AX)
                    ins = e.tensor_scalar(stat[:, 4:5], stat[:, 1:2], 1.0 / D,
                                          None, ALU.mult)
                    SY(e, svV, ins)
                    ins = e.tensor_scalar(stat[:, 2:3], stat[:, 0:1], 1.0 / D,
                                          None, ALU.mult)
                    SY(e, svV, ins)
                    ins = e.tensor_tensor(stat[:, 3:4], stat[:, 2:3],
                                          stat[:, 2:3], ALU.mult)
                    SY(e, svV, ins)
                    ins = e.tensor_tensor(stat[:, 5:6], stat[:, 4:5],
                                          stat[:, 3:4], ALU.subtract)
                    FND(e, cV, ins)
                    e.wait_ge(cA.h, b_c + 2 * t + 2)
                    ins = e.reciprocal(stat[:, 7:8], stat[:, 6:7])
                    SY(e, svV, ins)
                    if t >= 2:
                        e.wait_ge(*marks[t - 1])
                    ins = e.tensor_scalar(yt[t % 2], xs[t % 2], stat[:, 2:3],
                                          stat[:, 7:8], ALU.subtract, ALU.mult)
                    FND(e, cV, ins)

            @blk.sync
            def _(e):
                pass

            @blk.tensor
            def _(e):
                pass

            @blk.gpsimd
            def _(e):
                pass

    def emit_gateup(y_dram, st_y, use_int, cache_bf):
        with nc.Block() as blk:
            ar.reset()
            if use_int:
                wi_g = [ar.take([128, D], I32) for _ in range(2)]
                wi_u = [ar.take([128, D], I32) for _ in range(2)]
            wbg = [ar.take([128, D], BF16) for _ in range(2)]
            wbu = [ar.take([128, D], BF16) for _ in range(2)]
            tg = [ar.take([128, 1024], BF16) for _ in range(2)]
            yT = tarena
            b_p, b_v, b_a, b_o = cP.v, cV.v, cA.v, cO.v
            b_sb = sB[0].v
            b_cgu = st_cgu.v
            b_wl = [w.v for w in wl]
            w_y = st_y.v
            w_cgu_prev = st_cgu.v

            @blk.sync
            def _(e):
                e.wait_ge(st_y.h, w_y)
                if not use_int:
                    e.wait_ge(st_cgu.h, w_cgu_prev)
                for k in range(NDC):
                    e.dma_start(yT[:, ts(k, 1024)], y_dram[:, ts(k, 128)],
                                transpose=True).then_inc(sB[0].h, 16)
                sB[0].v += 16 * NDC
                for h in range(NHC):
                    if use_int:
                        if h >= 2:
                            e.wait_ge(cV.h, b_v + 2 * (h - 2) + 1)
                            e.wait_ge(cO.h, b_o + (h - 2) + 1)
                        e.dma_start(wi_g[h % 2], wg_t[h]).then_inc(
                            wl[h % 2].h, 16)
                        wl[h % 2].v += 16
                        e.dma_start(wi_u[h % 2], wu_t[h]).then_inc(
                            wl[2 + h % 2].h, 16)
                        wl[2 + h % 2].v += 16
                    else:
                        if h >= 2:
                            e.wait_ge(cP.h, b_p + (h - 2) + 1)
                        e.dma_start(wbg[h % 2], wg_bf[h]).then_inc(
                            wl[h % 2].h, 16)
                        wl[h % 2].v += 16
                        e.dma_start(wbu[h % 2], wu_bf[h]).then_inc(
                            wl[2 + h % 2].h, 16)
                        wl[2 + h % 2].v += 16

            @blk.vector
            def _(e):
                for h in range(NHC + 1):
                    if use_int and h < NHC:
                        e.wait_ge(wl[h % 2].h, b_wl[h % 2] + 16 * (h // 2 + 1))
                        if h >= 2:
                            e.wait_ge(cP.h, b_p + (h - 2) + 1)
                            if cache_bf:
                                e.wait_ge(st_cgu.h, b_cgu + 32 * h)
                        ins = e.tensor_scalar(wbg[h % 2], wi_g[h % 2], -128.0,
                                              None, ALU.add)
                        FIN(e, svV, cV, ins)
                    else:
                        e.nop().then_inc(cV.h, 1)
                        cV.v += 1
                    if h >= 1:
                        hh = h - 1
                        e.wait_ge(cA.h, b_a + 2 * (hh + 1))
                        ins = e.tensor_tensor(hid[:, ts(hh, 1024)],
                                              hid[:, ts(hh, 1024)],
                                              tg[hh % 2], ALU.mult)
                        FIN(e, svV, cV, ins)
                    else:
                        e.nop().then_inc(cV.h, 1)
                        cV.v += 1

            @blk.gpsimd
            def _(e):
                if use_int and cache_bf:
                    for h in range(NHC):
                        e.wait_ge(cV.h, b_v + 2 * h + 1)
                        e.wait_ge(cO.h, b_o + h + 1)
                        e.dma_start(wg_bf[h], wbg[h % 2]).then_inc(st_cgu.h, 16)
                        e.dma_start(wu_bf[h], wbu[h % 2]).then_inc(st_cgu.h, 16)
                        st_cgu.v += 32

            @blk.tensor
            def _(e):
                e.wait_ge(sB[0].h, b_sb + 16 * NDC)
                for h in range(NHC):
                    if use_int:
                        e.wait_ge(cV.h, b_v + 2 * h + 1)
                        e.wait_ge(cO.h, b_o + h + 1)
                        if cache_bf and h >= 2:
                            e.wait_ge(st_cgu.h, b_cgu + 32 * h)
                    else:
                        e.wait_ge(wl[h % 2].h, b_wl[h % 2] + 16 * (h // 2 + 1))
                        e.wait_ge(wl[2 + h % 2].h,
                                  b_wl[2 + h % 2] + 16 * (h // 2 + 1))
                    if h >= 2:
                        e.wait_ge(cA.h, b_a + 2 * (h - 1))
                    pg = (h % 2) * 2
                    pu = 4 + (h % 2) * 2
                    for k in range(NDC):
                        st_, sp = (k == 0), (k == NDC - 1)
                        wgs = wbg[h % 2][:, ts(k, 128)]
                        wus = wbu[h % 2][:, ts(k, 128)]
                        e.matmul(pb[pg][:, 0:512], wgs,
                                 yT[:, k * 1024:k * 1024 + 512],
                                 start=st_, stop=sp)
                        e.matmul(pb[pg + 1][:, 0:512], wgs,
                                 yT[:, k * 1024 + 512:(k + 1) * 1024],
                                 start=st_, stop=sp)
                        e.matmul(pb[pu][:, 0:512], wus,
                                 yT[:, k * 1024:k * 1024 + 512],
                                 start=st_, stop=sp)
                        ins = e.matmul(pb[pu + 1][:, 0:512], wus,
                                       yT[:, k * 1024 + 512:(k + 1) * 1024],
                                       start=st_, stop=sp)
                    ins.then_inc(cP.h, 1)
                    cP.v += 1

            @blk.scalar
            def _(e):
                for h in range(NHC + 1):
                    if use_int and h < NHC:
                        e.wait_ge(wl[2 + h % 2].h,
                                  b_wl[2 + h % 2] + 16 * (h // 2 + 1))
                        if h >= 2:
                            e.wait_ge(cP.h, b_p + (h - 2) + 1)
                            if cache_bf:
                                e.wait_ge(st_cgu.h, b_cgu + 32 * h)
                        ins = e.activation(wbu[h % 2], wi_u[h % 2], AF.Copy,
                                           bias=-128.0)
                        FIN(e, svA, cO, ins)
                    if h == 0:
                        continue
                    hh = h - 1
                    e.wait_ge(cP.h, b_p + hh + 1)
                    if hh >= 2:
                        e.wait_ge(cV.h, b_v + 2 * hh)
                    pg = (hh % 2) * 2
                    pu = 4 + (hh % 2) * 2
                    sgs = sgu_sb[:, hh:hh + 1]
                    sus = sgu_sb[:, NHC + hh:NHC + hh + 1]
                    e.activation(tg[hh % 2][:, 0:512], pb[pg][:, 0:512], AF.Silu,
                                 scale=sgs)
                    e.activation(tg[hh % 2][:, 512:1024], pb[pg + 1][:, 0:512],
                                 AF.Silu, scale=sgs)
                    e.activation(hid[:, hh * 1024:hh * 1024 + 512],
                                 pb[pu][:, 0:512], AF.Copy, scale=sus)
                    ins = e.activation(hid[:, hh * 1024 + 512:(hh + 1) * 1024],
                                       pb[pu + 1][:, 0:512], AF.Copy, scale=sus)
                    FIN(e, svA, cA, ins, 2)

    def emit_down(x_src, x1_dst, in_waits, use_int, cache_bf,
                  final=False, tag="a"):
        NKG = NHC // 4
        with nc.Block() as blk:
            ar.reset()
            if use_int:
                wdi = [ar.take([128, 2048], I32) for _ in range(2)]
            wdb = [ar.take([128, 2048], BF16) for _ in range(4)]
            xsl = [ar.take([128, 512], F32) for _ in range(4)]
            m1 = [ar.take([128, 512], F32) for _ in range(2)]
            x1sl = [ar.take([128, 512], F32) for _ in range(2)]
            b_p, b_v, b_a, b_cd = cP.v, cV.v, cA.v, st_cd.v
            b_wl = [w.v for w in wl]
            dxl = [Ctr(nc, f"dxl_{tag}{i}") for i in range(4)]
            stsl = [Ctr(nc, f"stsl_{tag}{i}") for i in range(2)]
            b_xl = [w.v for w in dxl]
            w_cgu_now = st_cgu.v
            iw = [(c.h, c.v) for c in (in_waits or [])]
            w_cd_prev = st_cd.v

            def wd_ap(dram, dq, g):
                return dram[dq, ts(g, 4)].transpose([1, 0, 2])

            def wd_sb3(t):
                return t.rearrange("p (g q) -> p g q", g=4)

            @blk.sync
            def _(e):
                e.wait_ge(st_cgu.h, w_cgu_now)
                if not use_int:
                    e.wait_ge(st_cd.h, w_cd_prev)
                for i in range(4 * NKG):
                    dq, g = divmod(i, NKG)
                    if use_int:
                        if i >= 2:
                            e.wait_ge(cA.h, b_a + (i - 2) + 1)
                        e.dma_start(wd_sb3(wdi[i % 2]),
                                    wd_ap(wd_t, dq, g)).then_inc(
                            wl[i % 2].h, 16)
                        wl[i % 2].v += 16
                    else:
                        if i >= 4:
                            e.wait_ge(cP.h, b_p + (i - 4) + 1)
                        e.dma_start(wd_sb3(wdb[i % 4]),
                                    wd_ap(wd_bf, dq, g)).then_inc(
                            wl[i % 4].h, 16)
                        wl[i % 4].v += 16

            @blk.scalar
            def _(e):
                if use_int:
                    for i in range(4 * NKG):
                        e.wait_ge(wl[i % 2].h, b_wl[i % 2] + 16 * (i // 2 + 1))
                        if i >= 4:
                            e.wait_ge(cP.h, b_p + (i - 4) + 1)
                        if cache_bf and i >= 4:
                            e.wait_ge(st_cd.h, b_cd + 16 * (i - 4 + 1))
                        ins = e.activation(wdb[i % 4], wdi[i % 2], AF.Copy,
                                           bias=-128.0)
                        FIN(e, svA, cA, ins)

            @blk.gpsimd
            def _(e):
                e.wait_ge(st_cgu.h, w_cgu_now)
                for hh_, vv_ in iw:
                    e.wait_ge(hh_, vv_)
                for dq in range(4):
                    if use_int and cache_bf:
                        for g in range(NKG):
                            i = dq * NKG + g
                            e.wait_ge(cA.h, b_a + i + 1)
                            e.dma_start(wd_ap(wd_bf, dq, g),
                                        wd_sb3(wdb[i % 4])).then_inc(st_cd.h, 16)
                            st_cd.v += 16
                    for t in range(8):
                        j = dq * 8 + t
                        if j >= 4:
                            e.wait_ge(cV.h, b_v + 2 * (j - 4) + 2)
                        e.dma_start(xsl[j % 4],
                                    x_src[ts(t, 128), ts(dq, 512)]).then_inc(
                                        dxl[j % 4].h, 16)
                        dxl[j % 4].v += 16
                        if j >= 2:
                            jj = j - 2
                            e.wait_ge(cV.h, b_v + 2 * jj + 2)
                            tq, tt_ = divmod(jj, 8)
                            e.dma_start(x1_dst[ts(tt_, 128), ts(tq, 512)],
                                        x1sl[jj % 2]).then_inc(stsl[jj % 2].h, 16)
                            stsl[jj % 2].v += 16
                for jj in [30, 31]:
                    e.wait_ge(cV.h, b_v + 2 * jj + 2)
                    tq, tt_ = divmod(jj, 8)
                    e.dma_start(x1_dst[ts(tt_, 128), ts(tq, 512)],
                                x1sl[jj % 2]).then_inc(stsl[jj % 2].h, 16)
                    stsl[jj % 2].v += 16
                if final:
                    for s_ in stsl:
                        e.wait_ge(s_.h, s_.v)

            @blk.tensor
            def _(e):
                for i in range(4 * NKG):
                    dq, g = divmod(i, NKG)
                    if use_int:
                        e.wait_ge(cA.h, b_a + i + 1)
                    else:
                        e.wait_ge(wl[i % 4].h, b_wl[i % 4] + 16 * (i // 4 + 1))
                    if g == 0 and dq >= 1:
                        e.wait_ge(cV.h, b_v + 16 * dq)
                    for kk in range(4):
                        k = g * 4 + kk
                        st_, sp = (k == 0), (k == NHC - 1)
                        for t in range(8):
                            ins = e.matmul(
                                pb[t][:, 0:512],
                                hid[:, k * 1024 + t * 128:k * 1024 + (t + 1) * 128],
                                wdb[i % 4][:, ts(kk, 512)], start=st_, stop=sp)
                    ins.then_inc(cP.h, 1)
                    cP.v += 1

            @blk.vector
            def _(e):
                for dq in range(4):
                    e.wait_ge(cP.h, b_p + NKG * (dq + 1))
                    for t in range(8):
                        j = dq * 8 + t
                        e.wait_ge(dxl[j % 4].h, b_xl[j % 4] + 16 * (j // 4 + 1))
                        if j >= 2:
                            e.wait_ge(stsl[j % 2].h, 16 * ((j - 2) // 2 + 1))
                        ins = e.tensor_tensor(m1[j % 2], pb[t][:, 0:512],
                                              sdb_sb[:, ts(dq, 512)], ALU.mult)
                        FIN(e, svV, cV, ins)
                        ins = e.tensor_tensor(x1sl[j % 2], m1[j % 2],
                                              xsl[j % 4], ALU.add)
                        FND(e, cV, ins)
        marks = {t: (stsl[t % 2].h, 16 * ((24 + t) // 2 + 1))
                 for t in range(8)}
        return stsl, marks

    if MK_PH != "p0":
        emit_ln(x_in, y0b, None, st_y0)
    if MK_PH not in ("p0", "ln"):
        emit_gateup(y0b, st_y0, use_int=True, cache_bf=True)
    if MK_PH not in ("p0", "ln", "gu"):
        x1_st, x1_marks = emit_down(x_in, x1b, None, use_int=True,
                                    cache_bf=False, tag='p1')
    if MK_PH not in ("p0", "ln", "gu", "dn"):
        emit_ln(x1b, y1c, None, st_y1, half_ctr=st_y1h, prog=x1_marks)

    if MK_PH in ("p0", "ln", "gu"):
        with nc.Block() as blk:
            @blk.gpsimd
            def _(e):
                e.dma_start(out_d, x_in).then_inc(sC.h, 16)
                sC.v += 16
                e.wait_ge(sC.h, sC.v)

            @blk.sync
            def _(e):
                pass

            @blk.scalar
            def _(e):
                pass

            @blk.vector
            def _(e):
                pass

            @blk.tensor
            def _(e):
                pass
        return nc

    if MK_PH in ("p1", "dn"):
        cpsrc = os.environ.get("MK_CP", "x1")
        with nc.Block() as blk:
            x1_waits = [(c.h, c.v) for c in x1_st]
            sStub = Ctr(nc, "sStub")

            @blk.gpsimd
            def _(e):
                for hh_, vv_ in x1_waits:
                    e.wait_ge(hh_, vv_)
                ob = out_d.bitcast(BF16)
                if cpsrc == "y0":
                    pairs = [(ob[:, 0:D], y0b)]
                elif cpsrc == "yt":
                    pairs = [(ob[0:128, :], tarena[:, 0:4096]),
                             (ob[128:256, :], tarena[:, 4096:8192])]
                elif cpsrc == "hid":
                    pairs = [(ob[0:128, :], hid[:, 0:4096]),
                             (ob[128:256, :], hid[:, 4096:8192]),
                             (ob[256:384, :], hid[:, 40960:45056])]
                else:
                    pairs = [(out_d, x1b)]
                for dst, srcp in pairs:
                    e.dma_start(dst, srcp).then_inc(sStub.h, 16)
                    sStub.v += 16
                e.wait_ge(sStub.h, sStub.v)

            @blk.sync
            def _(e):
                pass

            @blk.scalar
            def _(e):
                pass

            @blk.vector
            def _(e):
                pass

            @blk.tensor
            def _(e):
                pass
        return nc

    with nc.Block() as blk:
        w_y1 = st_y1.v
        w_y1h = st_y1h.v

        @blk.gpsimd
        def _(e):
            e.wait_ge(st_y1h.h, w_y1h)
            e.collective_compute("AllGather", ALU.bypass, rg,
                                 [y1c[0:512, :].opt()],
                                 [gy0.opt()]).then_inc(sC.h)
            sC.v += 1
            e.wait_ge(st_y1.h, w_y1)
            e.collective_compute("AllGather", ALU.bypass, rg,
                                 [y1c[512:1024, :].opt()],
                                 [gy1.opt()]).then_inc(sC.h)
            sC.v += 1

        @blk.sync
        def _(e):
            pass

        @blk.scalar
        def _(e):
            pass

        @blk.vector
        def _(e):
            pass

        @blk.tensor
        def _(e):
            pass

    def emit_proj():
        wqkv_sb = hid[:, 0:NDC * 768]
        with nc.Block() as blk:
            ar.reset()
            wqf = [ar.take([128, 768], F32) for _ in range(2)]
            wbf = [ar.take([128, 1], F32) for _ in range(2)]
            b_wl = [w.v for w in wl]
            b_a = cA.v

            @blk.sync
            def _(e):
                for k in range(NDC):
                    if k >= 2:
                        e.wait_ge(cA.h, b_a + 2 * (k - 2) + 2)
                    e.dma_start(wqf[k % 2], wqkv_in[k]).then_inc(wl[k % 2].h, 16)
                    wl[k % 2].v += 16
                    e.dma_start(wbf[k % 2], wb_in[k]).then_inc(wl[2 + k % 2].h, 16)
                    wl[2 + k % 2].v += 16

            @blk.scalar
            def _(e):
                for k in range(NDC):
                    e.wait_ge(wl[k % 2].h, b_wl[k % 2] + 16 * (k // 2 + 1))
                    e.wait_ge(wl[2 + k % 2].h, b_wl[2 + k % 2] + 16 * (k // 2 + 1))
                    e.activation(wqkv_sb[:, ts(k, 768)], wqf[k % 2], AF.Copy)
                    ins = e.activation(wb_sb[:, k:k + 1], wbf[k % 2], AF.Copy)
                    FIN(e, svA, cA, ins, 2)

            @blk.vector
            def _(e):
                pass

            @blk.tensor
            def _(e):
                pass

            @blk.gpsimd
            def _(e):
                pass

        with nc.Block() as blk:
            ar.reset()
            qraw = [[ar.take([128, 512], BF16) for _ in range(6)] for _ in range(2)]
            sil = [ar.take([128, 512], BF16) for _ in range(6)]
            sqb = [ar.take([128, 512], BF16) for _ in range(2)]
            qn_sb = [ar.take([128, 512], BF16) for _ in range(2)]
            kn_sb = [ar.take([128, 512], BF16) for _ in range(2)]
            rrow = ar.take([1, 512], F32)
            rinv = ar.take([1, 512], F32)
            betw = ar.take([1, 512], F32)
            b_p, b_v, b_a = cP.v, cV.v, cA.v
            b_sB = [s.v for s in sB]
            b_st = st_qkv.v
            w_coll = sC.v

            PW, VW, AW = 12, 10, 8

            @blk.sync
            def _(e):
                for w in range(NW):
                    if w == 0:
                        e.wait_ge(sC.h, w_coll - 1)
                    if w == 1:
                        e.wait_ge(sC.h, w_coll)
                    if w >= 2:
                        e.wait_ge(cP.h, b_p + PW * (w - 2) + 2)
                    src_g = gy0 if w % 2 == 0 else gy1
                    for k in range(NDC):
                        e.dma_start(
                            tarena[:, (w % 2) * 8192 + 512 * k:
                                   (w % 2) * 8192 + 512 * (k + 1)],
                            src_g[ts(w // 2, 512), ts(k, 128)],
                            transpose=True).then_inc(sB[w % 2].h, 16)
                    sB[w % 2].v += 256

            @blk.tensor
            def _(e):
                e.wait_ge(cA.h, b_a)
                for w in range(NW):
                    e.wait_ge(sB[w % 2].h, b_sB[w % 2] + 256 * (w // 2 + 1))
                    if w >= 1:
                        e.wait_ge(cA.h, b_a + AW * (w - 1) + 6)
                    slab = tarena[:, (w % 2) * 8192:(w % 2) * 8192 + 8192]
                    for k in range(NDC):
                        st_, sp = (k == 0), (k == NDC - 1)
                        rhs = slab[:, ts(k, 512)]
                        for et in range(6):
                            ins = e.matmul(pb[et][:, 0:512],
                                           wqkv_sb[:, 768 * k + 128 * et:
                                                   768 * k + 128 * (et + 1)],
                                           rhs, start=st_, stop=sp)
                    ins.then_inc(cP.h, 1)
                    cP.v += 1
                    if w >= 1:
                        e.wait_ge(cV.h, b_v + VW * w)
                    for k in range(NDC):
                        ins = e.matmul(pb[6][0:1, 0:512],
                                       wb_sb[:, k:k + 1], slab[:, ts(k, 512)],
                                       start=(k == 0), stop=(k == NDC - 1))
                    ins.then_inc(cP.h, 1)
                    cP.v += 1
                    for b in range(6):
                        e.wait_ge(cV.h, b_v + VW * w + b + 1)
                        raw = qraw[w % 2][b]
                        prv = qraw[(w - 1) % 2][b]
                        for s in range(4):
                            dg = cdiag[:, ts(4 * b + s, 128)]
                            if s >= 1 and w % 8 != 0:
                                e.matmul(pb[b][:, 0:s], dg,
                                         prv[:, 512 - s:512],
                                         start=False, stop=False)
                            ins = e.matmul(pb[b][:, s:512], dg,
                                           raw[:, 0:512 - s],
                                           start=(s == 0), stop=(s == 3))
                        ins.then_inc(cP.h, 1)
                        cP.v += 1
                    e.wait_ge(cA.h, b_a + AW * w + 3)
                    e.matmul(pb[7][0:1, 0:512], onesc_sb[:, 0:1], sqb[0],
                             start=True, stop=False)
                    ins = e.matmul(pb[7][0:1, 0:512], onesc_sb[:, 0:1], sqb[1],
                                   start=False, stop=True)
                    ins.then_inc(cP.h, 1)
                    cP.v += 1
                    e.wait_ge(cV.h, b_v + VW * w + 7)
                    ins = e.matmul(pb[6][:, 0:512], ones1f_sb[0:1, :],
                                   rinv[0:1, :], start=True, stop=True)
                    ins.then_inc(cP.h, 1)
                    cP.v += 1
                    e.wait_ge(cA.h, b_a + AW * w + 7)
                    e.matmul(pb[7][0:1, 0:512], onesc_sb[:, 0:1], sqb[0],
                             start=True, stop=False)
                    ins = e.matmul(pb[7][0:1, 0:512], onesc_sb[:, 0:1], sqb[1],
                                   start=False, stop=True)
                    ins.then_inc(cP.h, 1)
                    cP.v += 1
                    e.wait_ge(cV.h, b_v + VW * w + 9)
                    ins = e.matmul(pb[6][:, 0:512], ones1f_sb[0:1, :],
                                   rinv[0:1, :], start=True, stop=True)
                    ins.then_inc(cP.h, 1)
                    cP.v += 1

            @blk.vector
            def _(e):
                for w in range(NW):
                    if w >= 1:
                        e.wait_ge(st_qkv.h, b_st + 112 * w)
                    e.wait_ge(cP.h, b_p + PW * w + 1)
                    for b in range(6):
                        ins = e.tensor_copy(qraw[w % 2][b], pb[b][:, 0:512])
                        FND(e, cV, ins)
                    e.wait_ge(cA.h, b_a + AW * w + 7)
                    ins = e.reciprocal(rinv[0:1, :], rrow[0:1, :])
                    FND(e, cV, ins)
                    e.wait_ge(cP.h, b_p + PW * w + 10)
                    e.tensor_tensor(qn_sb[0], sil[0], pb[6][:, 0:512], ALU.mult)
                    ins = e.tensor_tensor(qn_sb[1], sil[1], pb[6][:, 0:512],
                                          ALU.mult)
                    FND(e, cV, ins)
                    e.wait_ge(cA.h, b_a + AW * w + 8)
                    ins = e.reciprocal(rinv[0:1, :], rrow[0:1, :])
                    FND(e, cV, ins)
                    e.wait_ge(cP.h, b_p + PW * w + 12)
                    e.tensor_tensor(kn_sb[0], sil[2], pb[6][:, 0:512], ALU.mult)
                    ins = e.tensor_tensor(kn_sb[1], sil[3], pb[6][:, 0:512],
                                          ALU.mult)
                    FND(e, cV, ins)

            @blk.scalar
            def _(e):
                for w in range(NW):
                    if w >= 1:
                        e.wait_ge(st_qkv.h, b_st + 112 * w)
                    e.wait_ge(cP.h, b_p + PW * w + 2)
                    ins = e.activation(betw[0:1, :], pb[6][0:1, 0:512],
                                       AF.Sigmoid)
                    FIN(e, svA, cA, ins)
                    e.wait_ge(cP.h, b_p + PW * w + 4)
                    e.activation(sil[0], pb[0][:, 0:512], AF.Silu)
                    ins = e.activation(sil[1], pb[1][:, 0:512], AF.Silu)
                    FIN(e, svA, cA, ins)
                    e.activation(sqb[0], sil[0], AF.Square)
                    ins = e.activation(sqb[1], sil[1], AF.Square)
                    FIN(e, svA, cA, ins)
                    e.wait_ge(cP.h, b_p + PW * w + 6)
                    e.activation(sil[2], pb[2][:, 0:512], AF.Silu)
                    ins = e.activation(sil[3], pb[3][:, 0:512], AF.Silu)
                    FIN(e, svA, cA, ins)
                    e.wait_ge(cP.h, b_p + PW * w + 9)
                    e.activation(sqb[0], sil[2], AF.Square)
                    ins = e.activation(sqb[1], sil[3], AF.Square)
                    FIN(e, svA, cA, ins)
                    e.wait_ge(cP.h, b_p + PW * w + 8)
                    e.activation(sil[4], pb[4][:, 0:512], AF.Silu)
                    ins = e.activation(sil[5], pb[5][:, 0:512], AF.Silu)
                    FIN(e, svA, cA, ins)
                    e.wait_ge(cP.h, b_p + PW * w + 9)
                    ins = e.activation(rrow[0:1, :], pb[7][0:1, 0:512], AF.Sqrt,
                                       scale=fc_sb[0:1, 3:4],
                                       bias=fc_sb[0:1, 1:2])
                    FIN(e, svA, cA, ins)
                    e.wait_ge(cP.h, b_p + PW * w + 11)
                    e.wait_ge(cV.h, b_v + VW * w + 7)
                    ins = e.activation(rrow[0:1, :], pb[7][0:1, 0:512], AF.Sqrt,
                                       scale=fc_sb[0:1, 5:6],
                                       bias=fc_sb[0:1, 6:7])
                    FIN(e, svA, cA, ins)
                    e.dma_start(beta_d[0:1, ts(w, 512)], betw[0:1, :]
                                ).then_inc(st_qkv.h, 16)
                    e.dma_start(v_s[0:128, ts(w, 512)], sil[4]
                                ).then_inc(st_qkv.h, 16)
                    e.dma_start(v_s[128:256, ts(w, 512)], sil[5]
                                ).then_inc(st_qkv.h, 16)
                    e.wait_ge(cV.h, b_v + VW * w + 8)
                    e.dma_start(q_s[0:128, ts(w, 512)], qn_sb[0]
                                ).then_inc(st_qkv.h, 16)
                    e.dma_start(q_s[128:256, ts(w, 512)], qn_sb[1]
                                ).then_inc(st_qkv.h, 16)
                    e.wait_ge(cV.h, b_v + VW * w + 10)
                    e.dma_start(k_s[0:128, ts(w, 512)], kn_sb[0]
                                ).then_inc(st_qkv.h, 16)
                    e.dma_start(k_s[128:256, ts(w, 512)], kn_sb[1]
                                ).then_inc(st_qkv.h, 16)
                    st_qkv.v += 112

    emit_proj()

    if MK_PH == "p3":
        with nc.Block() as blk:
            w_qkv = st_qkv.v
            sStub3 = Ctr(nc, "sStub3")
            cps = os.environ.get("MK_CP", "qkv")

            @blk.gpsimd
            def _(e):
                e.wait_ge(st_qkv.h, w_qkv)
                vw = out_d.bitcast(BF16).rearrange("(a b) c -> a (b c)", b=2)
                if cps == "qkv2":
                    pairs = [(vw[0:256, :], v_s),
                             (out_d.rearrange("(a b) c -> a (b c)", b=4)[128:129, :],
                              beta_d)]
                else:
                    pairs = [(vw[0:256, :], q_s), (vw[256:512, :], k_s)]
                for dst, srcp in pairs:
                    e.dma_start(dst, srcp).then_inc(sStub3.h, 16)
                    sStub3.v += 16
                e.wait_ge(sStub3.h, sStub3.v)

            @blk.sync
            def _(e):
                pass

            @blk.scalar
            def _(e):
                pass

            @blk.vector
            def _(e):
                pass

            @blk.tensor
            def _(e):
                pass
        return nc

    def emit_delta():
        ss_q, ss_k, ss_kt, ss_vt = 0, 1024, 2048, 3072
        NGE = int(os.environ.get("MK_NG", NG))

        def prep_ops(g):
            ops = [('T', ('bT', g)), ('V', ('bp', g)), ('V', ('nbv', g))]
            for l in range(4):
                ops += [('T', ('G0', l, g)), ('V', ('A', l, g))]
            for l in range(4):
                ops += [('T', ('AT', l, g)), ('V', ('init', l, g))]
            for l in range(4):
                ops += [('T', ('W2', l, g)), ('V', ('eLt', l, g))]
            for it in range(6):
                ops += [('T', ('P1', it, g)), ('V', ('eP2', it, g)),
                        ('T', ('X', it, g)), ('A', ('eX', it, g)),
                        ('T', ('XT', it, g)), ('A', ('eXT', it, g))]
            for l in range(4):
                ops += [('T', ('nG', l, g)), ('A', ('enG', l, g))]
            return ops

        def chain_ops(g):
            ops = []
            for l in range(4):
                ops += [('T', ('U', l, g)), ('A', ('eU', l, g)),
                        ('T', ('O', l, g)), ('V', ('eof', l, g)),
                        ('A', ('rms', l, g)), ('V', ('eon', l, g)),
                        ('A', ('st', l, g)),
                        ('T', ('Su', l, g)), ('V', ('eS', l, g))]
            return ops

        weave = [('V', ('S0',))]
        for g in range(NGE):
            prep = prep_ops(g)
            chain = chain_ops(g - 1) if g >= 1 else []
            ci = 0
            for idx, p in enumerate(prep):
                weave.append(p)
                while ci < len(chain) and \
                        (idx + 1) * len(chain) // len(prep) > ci:
                    weave.append(chain[ci])
                    ci += 1
            weave += chain[ci:]
        weave += chain_ops(NGE - 1)

        with nc.Block() as blk:
            ar.reset()
            Xb2 = [ar.take([128, 512], BF16) for _ in range(2)]
            Xtb = ar.take([128, 512], BF16)
            P2b = ar.take([128, 512], BF16)
            A_sb = ar.take([128, 512], BF16)
            AT_sb = ar.take([128, 512], BF16)
            Lt2 = [ar.take([128, 512], BF16) for _ in range(2)]
            negG2 = [ar.take([128, 1024], BF16) for _ in range(2)]
            nbk2 = [ar.take([128, 1024], BF16) for _ in range(2)]
            bV2 = [ar.take([128, 1024], BF16) for _ in range(2)]
            U_sb = ar.take([128, 1024], BF16)
            S_sb = ar.take([128, 1024], BF16)
            S_f32 = ar.take([128, 1024], F32)
            o_n = ar.take([128, 1024], BF16)
            o_f = ar.take([128, 256], F32)
            scr2 = ar.take([128, 256], F32)
            osum = ar.take([128, 4], F32)
            osr = ar.take([128, 4], F32)
            bp = ar.take([128, 4], F32)
            bet_row = ar.take([1, 1024], F32)

            b_sB = [s.v for s in sB]
            st_ol = [Ctr(nc, f"st_ol{i}") for i in range(4)]
            w_qkv = st_qkv.v

            pb7b = pb[7].bitcast(BF16)
            pb3b = pb[3].bitcast(BF16)
            pb4b = pb[4].bitcast(BF16)

            posP, posV, posA, posS = {}, {}, {}, {}
            pP, pV, pA = cP.v, cV.v, cA.v
            pS = [c.v for c in st_ol]
            for eng, key in weave:
                if eng == 'T':
                    pP += 1
                    posP[key] = pP
                elif eng == 'V':
                    pV += 1
                    posV[key] = pV
                elif key[0] in ('rms', 'eX', 'eXT', 'eU', 'enG'):
                    pA += 2 if key[0] == 'rms' else 1
                    posA[key] = pA
                else:
                    l, g = key[1], key[2]
                    pS[l] += 16
                    posS[(l, g)] = (st_ol[l].h, pS[l])

            def lane_bi(l):
                return l % 2, (1 if l >= 2 else 0)

            def kq_tile(buf, l, h):
                return buf[:, 512 * (l % 2) + 256 * h + 128 * (l // 2):
                           512 * (l % 2) + 256 * h + 128 * (l // 2) + 128]

            @blk.sync
            def _(e):
                e.wait_ge(st_qkv.h, w_qkv)
                for g in range(NGE):
                    s = g % 2
                    if g >= 2:
                        e.wait_ge(cP.h, posP[('Su', 3, g - 2)])
                        e.wait_ge(cV.h, posV[('eS', 3, g - 2)])
                    for b in range(2):
                        for h in range(2):
                            e.dma_start(
                                tarena[:, s * 8192 + ss_q + 512 * b + 256 * h:
                                       s * 8192 + ss_q + 512 * b + 256 * h + 256],
                                q_s[128 * h:128 * (h + 1),
                                    4096 * b + 256 * g:4096 * b + 256 * (g + 1)]
                            ).then_inc(sB[s].h, 16)
                            e.dma_start(
                                tarena[:, s * 8192 + ss_k + 512 * b + 256 * h:
                                       s * 8192 + ss_k + 512 * b + 256 * h + 256],
                                k_s[128 * h:128 * (h + 1),
                                    4096 * b + 256 * g:4096 * b + 256 * (g + 1)]
                            ).then_inc(sB[s].h, 16)
                    for l in range(4):
                        b, i = lane_bi(l)
                        cc = 2 * g + i
                        e.dma_start(
                            tarena[:, s * 8192 + ss_kt + 256 * l:
                                   s * 8192 + ss_kt + 256 * (l + 1)],
                            k_s[:, 4096 * b + 128 * cc:4096 * b + 128 * (cc + 1)],
                            transpose=True).then_inc(sB[s].h, 16)
                        e.dma_start(
                            tarena[:, s * 8192 + ss_vt + 256 * l:
                                   s * 8192 + ss_vt + 256 * (l + 1)],
                            v_s[:, 4096 * b + 128 * cc:4096 * b + 128 * (cc + 1)],
                            transpose=True).then_inc(sB[s].h, 16)
                    for b in range(2):
                        e.dma_start(
                            bet_row[0:1, 512 * s + 256 * b:512 * s + 256 * (b + 1)],
                            beta_d[0:1, 4096 * b + 256 * g:4096 * b + 256 * (g + 1)]
                        ).then_inc(sB[s].h, 16)
                    sB[s].v += 18 * 16

            @blk.tensor
            def _(e):
                for eng, key in weave:
                    if eng != 'T':
                        continue
                    nm, g = key[0], key[-1]
                    s = g % 2
                    kch = tarena[:, s * 8192 + ss_k:s * 8192 + ss_k + 1024]
                    qch = tarena[:, s * 8192 + ss_q:s * 8192 + ss_q + 1024]
                    if nm == 'bT':
                        e.wait_ge(sB[s].h, b_sB[s] + 288 * (g // 2 + 1))
                        if g >= 1:
                            e.wait_ge(cV.h, posV[('init', 3, g - 1)])
                        for l in range(4):
                            b, i = lane_bi(l)
                            ins = e.transpose(
                                pb[7][:, 256 + l:256 + l + 1],
                                bet_row[0:1, 512 * s + 256 * b + 128 * i:
                                        512 * s + 256 * b + 128 * (i + 1)],
                                identf_sb[0:1, 0:1])
                    elif nm == 'G0':
                        l = key[1]
                        if l == 0 and g >= 1:
                            e.wait_ge(cA.h, posA[('enG', 3, g - 1)])
                        if l >= 2:
                            e.wait_ge(cV.h, posV[('A', l - 2, g)])
                        for h in range(2):
                            ins = e.matmul(pb[2 + l % 2][:, 0:128],
                                           kq_tile(kch, l, h),
                                           kq_tile(kch, l, h),
                                           start=(h == 0), stop=(h == 1))
                    elif nm == 'AT':
                        l = key[1]
                        e.wait_ge(cV.h, posV[('A', l, g)])
                        if l < 2:
                            e.wait_ge(cV.h, posV[('bp', g)])
                        else:
                            e.wait_ge(cV.h, posV[('init', l - 2, g)])
                        dst = pb7b if l % 2 == 0 else pb4b
                        ins = e.transpose(dst[:, 0:128],
                                          A_sb[:, ts(l, 128)], ident)
                    elif nm == 'W2':
                        l = key[1]
                        e.wait_ge(cV.h, posV[('A', 2 + l % 2, g)])
                        if l >= 2:
                            e.wait_ge(cV.h, posV[('eLt', l - 2, g)])
                        for h in range(2):
                            ins = e.matmul(pb[2 + l % 2][:, 0:128],
                                           kq_tile(kch, l, h),
                                           kq_tile(qch, l, h),
                                           start=(h == 0), stop=(h == 1))
                    elif nm == 'P1':
                        it = key[1]
                        if it == 0:
                            e.wait_ge(cV.h, posV[('init', 3, g)])
                            e.wait_ge(cV.h, posV[('eLt', 3, g)])
                        else:
                            e.wait_ge(cA.h, posA[('eXT', it - 1, g)])
                        for l in range(4):
                            e.matmul(pb[3][:, ts(l, 128)], ident,
                                     Xb2[s][:, ts(l, 128)],
                                     start=True, stop=False)
                            ins = e.matmul(pb[3][:, ts(l, 128)],
                                           A_sb[:, ts(l, 128)],
                                           Xb2[s][:, ts(l, 128)],
                                           start=False, stop=True)
                    elif nm == 'X':
                        it = key[1]
                        e.wait_ge(cV.h, posV[('eP2', it, g)])
                        for l in range(4):
                            ins = e.matmul(pb[4][:, ts(l, 128)],
                                           Xtb[:, ts(l, 128)],
                                           P2b[:, ts(l, 128)],
                                           start=True, stop=True)
                    elif nm == 'XT':
                        it = key[1]
                        e.wait_ge(cA.h, posA[('eX', it, g)])
                        for l in range(4):
                            ins = e.transpose(pb3b[:, ts(l, 128)],
                                              Xb2[s][:, ts(l, 128)], ident)
                    elif nm == 'nG':
                        l = key[1]
                        if l == 0:
                            e.wait_ge(cV.h, posV[('eLt', 2, g)])
                            e.wait_ge(cA.h, posA[('eX', 5, g)])
                        elif l == 1:
                            e.wait_ge(cV.h, posV[('eLt', 3, g)])
                            e.wait_ge(cA.h, posA[('eXT', 5, g)])
                        else:
                            e.wait_ge(cA.h, posA[('enG', l - 2, g)])
                        for h in range(2):
                            ins = e.matmul(pb[2 + l % 2][:, 128 * h:
                                                         128 * (h + 1)],
                                           nbk2[s][:, 256 * l + 128 * h:
                                                   256 * l + 128 * (h + 1)],
                                           Xb2[s][:, ts(l, 128)],
                                           start=True, stop=True)
                    elif nm == 'U':
                        l = key[1]
                        sq = l % 2
                        e.wait_ge(cA.h, posA[('enG', l, g)])
                        if l >= 2:
                            e.wait_ge(cV.h, posV[('eS', l - 2, g)])
                        elif g >= 1:
                            e.wait_ge(cV.h, posV[('eS', l + 2, g - 1)])
                            if l == 0:
                                e.wait_ge(cA.h, posA[('eU', 3, g - 1)])
                        else:
                            e.wait_ge(cV.h, posV[('S0',)])
                        e.matmul(pb[5][:, 0:256], Xb2[s][:, ts(l, 128)],
                                 bV2[s][:, ts(l, 256)], start=True, stop=False)
                        for h in range(2):
                            ins = e.matmul(pb[5][:, 0:256],
                                           negG2[s][:, 256 * l + 128 * h:
                                                    256 * l + 128 * (h + 1)],
                                           S_sb[:, 512 * sq + 256 * h:
                                                512 * sq + 256 * (h + 1)],
                                           start=False, stop=(h == 1))
                    elif nm == 'O':
                        l = key[1]
                        sq = l % 2
                        e.wait_ge(cA.h, posA[('eU', l, g)])
                        if l == 0 and g >= 1:
                            e.wait_ge(cV.h, posV[('eof', 3, g - 1)])
                        for h in range(2):
                            e.matmul(pb[6][:, 0:256], kq_tile(qch, l, h),
                                     S_sb[:, 512 * sq + 256 * h:
                                          512 * sq + 256 * (h + 1)],
                                     start=(h == 0), stop=False)
                        ins = e.matmul(pb[6][:, 0:256], Lt2[s][:, ts(l, 128)],
                                       U_sb[:, ts(l, 256)],
                                       start=False, stop=True)
                    elif nm == 'Su':
                        l = key[1]
                        sq = l % 2
                        for h in range(2):
                            ins = e.matmul(
                                pb[sq][:, ts(h, 256)],
                                tarena[:, s * 8192 + ss_kt + 256 * l + 128 * h:
                                       s * 8192 + ss_kt + 256 * l + 128 * (h + 1)],
                                U_sb[:, ts(l, 256)],
                                start=(h == 0), stop=(h == 1))
                    ins.then_inc(cP.h, 1)
                    cP.v += 1
                    assert cP.v == posP[key], (key, cP.v, posP[key])

            @blk.vector
            def _(e):
                for eng, key in weave:
                    if eng != 'V':
                        continue
                    nm = key[0]
                    if nm == 'S0':
                        e.wait_ge(st_qkv.h, w_qkv)
                        e.memset(S_sb[:, :], 0)
                        ins = e.memset(S_f32[:, :], 0)
                        FIN(e, svV, cV, ins)
                        assert cV.v == posV[key]
                        continue
                    g = key[-1]
                    s = g % 2
                    ktok = tarena[:, s * 8192 + ss_kt:s * 8192 + ss_kt + 1024]
                    vtok = tarena[:, s * 8192 + ss_vt:s * 8192 + ss_vt + 1024]
                    if nm == 'bp':
                        e.wait_ge(cP.h, posP[('bT', g)])
                        ins = e.tensor_copy(bp[:, 0:4], pb[7][:, 256:260])
                        FIN(e, svV, cV, ins)
                    elif nm == 'nbv':
                        for l in range(4):
                            e.tensor_scalar(nbk2[s][:, ts(l, 256)],
                                            ktok[:, ts(l, 256)],
                                            bp[:, l:l + 1], -1.0,
                                            ALU.mult, ALU.mult)
                            ins = e.tensor_scalar(bV2[s][:, ts(l, 256)],
                                                  vtok[:, ts(l, 256)],
                                                  bp[:, l:l + 1], None,
                                                  ALU.mult)
                        FND(e, cV, ins)
                    elif nm == 'A':
                        l = key[1]
                        e.wait_ge(cP.h, posP[('G0', l, g)])
                        ins = e.scalar_tensor_tensor(A_sb[:, ts(l, 128)],
                                                     pb[2 + l % 2][:, 0:128],
                                                     bp[:, l:l + 1], slm,
                                                     ALU.mult, ALU.mult)
                        FND(e, cV, ins)
                    elif nm == 'init':
                        l = key[1]
                        e.wait_ge(cP.h, posP[('AT', l, g)])
                        ins = e.tensor_copy(AT_sb[:, ts(l, 128)],
                                            (pb7b if l % 2 == 0 else
                                             pb4b)[:, 0:128])
                        SY(e, svV, ins)
                        e.tensor_tensor(Xb2[s][:, ts(l, 128)], ident,
                                        AT_sb[:, ts(l, 128)], ALU.subtract)
                        ins = e.tensor_tensor(Xtb[:, ts(l, 128)], ident,
                                              A_sb[:, ts(l, 128)],
                                              ALU.subtract)
                        FND(e, cV, ins)
                    elif nm == 'eP2':
                        e.wait_ge(cP.h, posP[('P1', key[1], g)])
                        ins = e.tensor_tensor(P2b[:, :], twoI, pb[3][:, 0:512],
                                              ALU.subtract)
                        FND(e, cV, ins)
                    elif nm == 'eLt':
                        l = key[1]
                        e.wait_ge(cP.h, posP[('W2', l, g)])
                        ins = e.tensor_tensor(Lt2[s][:, ts(l, 128)],
                                              pb[2 + l % 2][:, 0:128], ium,
                                              ALU.mult)
                        FND(e, cV, ins)
                    elif nm == 'eof':
                        l = key[1]
                        e.wait_ge(cP.h, posP[('O', l, g)])
                        ins = e.tensor_copy(o_f[:, :], pb[6][:, 0:256])
                        FND(e, cV, ins)
                    elif nm == 'eon':
                        l = key[1]
                        e.wait_ge(cA.h, posA[('rms', l, g)])
                        if g >= 1:
                            e.wait_ge(*posS[(l, g - 1)])
                        ins = e.reciprocal(osr[:, 1:2], osr[:, 0:1])
                        SY(e, svV, ins)
                        ins = e.tensor_scalar(o_n[:, ts(l, 256)], o_f[:, :],
                                              osr[:, 1:2], None, ALU.mult)
                        FND(e, cV, ins)
                    elif nm == 'eS':
                        l = key[1]
                        sq = l % 2
                        e.wait_ge(cP.h, posP[('Su', l, g)])
                        ins = e.tensor_tensor(S_f32[:, ts(sq, 512)],
                                              S_f32[:, ts(sq, 512)],
                                              pb[sq][:, 0:512], ALU.add)
                        SY(e, svV, ins)
                        ins = e.tensor_copy(S_sb[:, ts(sq, 512)],
                                            S_f32[:, ts(sq, 512)])
                        FND(e, cV, ins)
                    assert cV.v == posV[key], (key, cV.v, posV[key])

            @blk.scalar
            def _(e):
                for eng, key in weave:
                    if eng != 'A':
                        continue
                    nm, l, g = key
                    s = g % 2
                    if nm == 'eX':
                        e.wait_ge(cP.h, posP[('X', l, g)])
                        ins = e.activation(Xb2[s][:, :], pb[4][:, 0:512],
                                           AF.Copy)
                        FND(e, cA, ins)
                        assert cA.v == posA[key]
                        continue
                    if nm == 'eXT':
                        e.wait_ge(cP.h, posP[('XT', l, g)])
                        ins = e.activation(Xtb[:, :], pb3b[:, 0:512], AF.Copy)
                        FND(e, cA, ins)
                        assert cA.v == posA[key]
                        continue
                    if nm == 'eU':
                        e.wait_ge(cP.h, posP[('U', l, g)])
                        ins = e.activation(U_sb[:, ts(l, 256)],
                                           pb[5][:, 0:256], AF.Copy)
                        FND(e, cA, ins)
                        assert cA.v == posA[key]
                        continue
                    if nm == 'enG':
                        e.wait_ge(cP.h, posP[('nG', l, g)])
                        ins = e.activation(negG2[s][:, ts(l, 256)],
                                           pb[2 + l % 2][:, 0:256], AF.Copy)
                        FND(e, cA, ins)
                        assert cA.v == posA[key]
                        continue
                    if nm == 'rms':
                        e.wait_ge(cV.h, posV[('eof', l, g)])
                        ins = e.activation(scr2[:, :], o_f[:, :], AF.Square,
                                           accum_out=osum[:, 0:1])
                        SY(e, svA, ins)
                        ins = e.activation(osr[:, 0:1], osum[:, 0:1], AF.Sqrt,
                                           scale=fc_sb[:, 4:5],
                                           bias=fc_sb[:, 2:3])
                        FIN(e, svA, cA, ins, 2)
                        assert cA.v == posA[key], (key, cA.v, posA[key])
                    else:
                        b, i = lane_bi(l)
                        cc = 2 * g + i
                        e.wait_ge(cV.h, posV[('eon', l, g)])
                        e.dma_start(
                            o_c[4096 * b + 128 * cc:4096 * b + 128 * (cc + 1), :],
                            o_n[:, ts(l, 256)]).then_inc(st_ol[l].h, 16)
                        st_ol[l].v += 16

            @blk.gpsimd
            def _(e):
                if NGE == NG:
                    for c_ in st_ol:
                        e.wait_ge(c_.h, posS[(st_ol.index(c_), 7)][1])
                    for r0 in (0, 4096):
                        e.collective_compute(
                            "AllGather", ALU.bypass, rg,
                            [o_c[r0:r0 + 2048, :].opt()],
                            [go[(r0 // 4096) * 16384:
                                (r0 // 4096) * 16384 + 16384, :].opt()]
                        ).then_inc(sC.h)
                        sC.v += 1
                    for c_ in st_ol:
                        e.wait_ge(c_.h, c_.v)
                    for r0 in (2048, 6144):
                        e.collective_compute(
                            "AllGather", ALU.bypass, rg,
                            [o_c[r0:r0 + 2048, :].opt()],
                            [go[32768 + (r0 // 4096) * 16384:
                                32768 + (r0 // 4096) * 16384 + 16384, :].opt()]
                        ).then_inc(sC.h)
                        sC.v += 1
        return st_ol

    st_ol_g = emit_delta()

    if MK_PH == "p4":
        with nc.Block() as blk:
            w_ol = [(c.h, c.v) for c in st_ol_g]
            sStub4 = Ctr(nc, "sStub4")

            @blk.gpsimd
            def _(e):
                for hh_, vv_ in w_ol:
                    e.wait_ge(hh_, vv_)
                e.dma_start(out_d.bitcast(BF16)[:, 0:D],
                            o_c.rearrange("(a b) c -> a (b c)", b=8)
                            ).then_inc(sStub4.h, 16)
                sStub4.v += 16
                e.wait_ge(sStub4.h, sStub4.v)

            @blk.sync
            def _(e):
                pass

            @blk.scalar
            def _(e):
                pass

            @blk.vector
            def _(e):
                pass

            @blk.tensor
            def _(e):
                pass
        return nc

    def emit_outproj():
        with nc.Block() as blk:
            ar.reset()
            wof = [ar.take([128, D], F32) for _ in range(2)]
            x1sl = [ar.take([128, D], F32) for _ in range(2)]
            x2sl = [ar.take([128, D], F32) for _ in range(2)]
            sGo = Ctr(nc, "sGo")
            stx2 = [Ctr(nc, "stx2_0"), Ctr(nc, "stx2_1")]
            wo_bf = hid[:, 0:NDC * D]
            b_wl = [w.v for w in wl]
            b_xl = [w.v for w in xl]
            b_p, b_v, b_a = cP.v, cV.v, cA.v
            w_coll = sC.v
            x1w = [(c.h, c.v) for c in x1_st]

            @blk.sync
            def _(e):
                e.wait_ge(sC.h, w_coll)
                for h in range(H):
                    sv = e.value_load(ooff_sb[h:h + 1, 0:1])
                    blk_ap = dataclasses.replace(go[0:NT, :], offset=sv)
                    e.dma_start(o_m[ts(h, NT), :], blk_ap).then_inc(sGo.h, 16)
                    sGo.v += 16
                e.wait_ge(sGo.h, sGo.v)
                for h in range(H):
                    for e2 in range(2):
                        k = 2 * h + e2
                        e.dma_start(tarena[:, ts(k, 1024)],
                                    o_m[ts(h, NT), ts(e2, 128)],
                                    transpose=True).then_inc(sGo.h, 16)
                        sGo.v += 16

            @blk.scalar
            def _(e):
                for k in range(2):
                    e.dma_start(wof[k], wo_in[ts(k, 128), :]).then_inc(
                        wl[k].h, 16)
                    wl[k].v += 16
                for k in range(NDC):
                    e.wait_ge(wl[k % 2].h, b_wl[k % 2] + 16 * (k // 2 + 1))
                    ins = e.activation(wo_bf[:, ts(k, D)], wof[k % 2], AF.Copy,
                                       scale=nrmw_sb[:, k:k + 1])
                    FIN(e, svA, cA, ins)
                    if k + 2 < NDC:
                        e.dma_start(wof[k % 2], wo_in[ts(k + 2, 128), :]
                                    ).then_inc(wl[k % 2].h, 16)
                        wl[k % 2].v += 16
                for hh_, vv_ in x1w:
                    e.wait_ge(hh_, vv_)
                for t in range(8):
                    if t >= 2:
                        e.wait_ge(cV.h, b_v + 4 * (t - 1))
                    e.dma_start(x1sl[t % 2], x1b[ts(t, 128), :]).then_inc(
                        xl[t % 2].h, 16)
                    xl[t % 2].v += 16

            @blk.tensor
            def _(e):
                e.wait_ge(sGo.h, sGo.v)
                for t in range(8):
                    if t >= 2:
                        e.wait_ge(cV.h, b_v + 4 * (t - 1))
                    pbs = 0 if t % 2 == 0 else 4
                    for k in range(NDC):
                        if t == 0:
                            e.wait_ge(cA.h, b_a + k + 1)
                        for dq in range(4):
                            ins = e.matmul(
                                pb[pbs + dq][:, 0:512],
                                tarena[:, 1024 * k + 128 * t:
                                       1024 * k + 128 * (t + 1)],
                                wo_bf[:, D * k + 512 * dq:D * k + 512 * (dq + 1)],
                                start=(k == 0), stop=(k == NDC - 1))
                    ins.then_inc(cP.h, 1)
                    cP.v += 1

            @blk.vector
            def _(e):
                for t in range(8):
                    e.wait_ge(cP.h, b_p + t + 1)
                    e.wait_ge(xl[t % 2].h, b_xl[t % 2] + 16 * (t // 2 + 1))
                    if t >= 2:
                        e.wait_ge(stx2[t % 2].h, 16 * (t // 2))
                    pbs = 0 if t % 2 == 0 else 4
                    for dq in range(4):
                        ins = e.tensor_tensor(x2sl[t % 2][:, ts(dq, 512)],
                                              pb[pbs + dq][:, 0:512],
                                              x1sl[t % 2][:, ts(dq, 512)],
                                              ALU.add)
                        FND(e, cV, ins)

            @blk.gpsimd
            def _(e):
                for t in range(8):
                    e.wait_ge(cV.h, b_v + 4 * t + 4)
                    e.dma_start(x2b[ts(t, 128), :],
                                x2sl[t % 2]).then_inc(stx2[t % 2].h, 16)
                    stx2[t % 2].v += 16
        return stx2

    x2_st = emit_outproj()
    emit_ln(x2b, y2b, x2_st, st_y2)

    emit_gateup(y2b, st_y2, use_int=False, cache_bf=False)
    emit_down(x2b, out_d, x2_st, use_int=True, cache_bf=False,
              final=True, tag='p6')

    if MK_DBG:
        src_map = {"x1": x1b, "y1": None, "x2": x2b}
        dsrc = src_map.get(MK_DBG, x1b)
        with nc.Block() as blk:
            @blk.gpsimd
            def _(e):
                e.dma_start(dbg_d, dsrc).then_inc(sC.h, 16)
                sC.v += 16
                e.wait_ge(sC.h, sC.v)

            @blk.sync
            def _(e):
                pass

            @blk.scalar
            def _(e):
                pass

            @blk.vector
            def _(e):
                pass

            @blk.tensor
            def _(e):
                pass

    return nc


_NC_CACHE = {}


def _host_layout(inputs):
    x = np.asarray(inputs["L0_attn_hidden_states"])
    wq = np.asarray(inputs["L0_attn_Wq"])
    wk = np.asarray(inputs["L0_attn_Wk"])
    wv = np.asarray(inputs["L0_attn_Wv"])
    wb = np.asarray(inputs["L0_attn_Wb"])
    cq = np.asarray(inputs["L0_attn_conv_q"])
    ck = np.asarray(inputs["L0_attn_conv_k"])
    cv = np.asarray(inputs["L0_attn_conv_v"])
    nw = np.asarray(inputs["L0_attn_norm_w"])
    wo = np.asarray(inputs["L0_attn_Wo"])
    wqg = np.asarray(inputs["L0_ffn_wq_gate"])
    sg = np.asarray(inputs["L0_ffn_s_gate"])
    wqu = np.asarray(inputs["L0_ffn_wq_up"])
    su = np.asarray(inputs["L0_ffn_s_up"])
    wqd = np.asarray(inputs["L0_ffn_wq_down"])
    sd = np.asarray(inputs["L0_ffn_s_down"])

    bf = ml_dtypes.bfloat16
    xf = np.ascontiguousarray(x.reshape(GT, D)).astype(np.float32)
    wg_t = np.ascontiguousarray(
        wqg.reshape(NHC, 128, NDC, 128).transpose(0, 3, 2, 1).reshape(NHC, 128, D)
    ).astype(np.int32)
    wu_t = np.ascontiguousarray(
        wqu.reshape(NHC, 128, NDC, 128).transpose(0, 3, 2, 1).reshape(NHC, 128, D)
    ).astype(np.int32)
    wd_t = np.ascontiguousarray(
        wqd.reshape(4, 512, NHC, 128).transpose(0, 2, 3, 1)
    ).astype(np.int32)
    sg_in = np.ascontiguousarray(sg.reshape(NHC, 128).T).astype(np.float32)
    su_in = np.ascontiguousarray(su.reshape(NHC, 128).T).astype(np.float32)
    sd_in = np.ascontiguousarray(sd.reshape(1, D)).astype(np.float32)
    nrmw_in = np.ascontiguousarray(
        np.tile(nw, NCORE).reshape(NDC, 128).T).astype(np.float32)
    wo_in = np.ascontiguousarray(wo).astype(np.float32)

    ident = np.eye(128, dtype=bf)
    identf = np.eye(128, dtype=np.float32)
    twoI = np.ascontiguousarray(
        np.tile(2.0 * np.eye(128, dtype=np.float32), (1, 4))).astype(bf)
    ii, jj = np.meshgrid(np.arange(128), np.arange(128), indexing="ij")
    slmask = (ii > jj).astype(bf)
    iumask = (ii <= jj).astype(bf)
    ones1 = np.ones((1, 128), dtype=bf)
    half1 = np.full((1, 128), 0.5, dtype=bf)

    in_maps = []
    for j in range(NCORE):
        wqkv = np.concatenate(
            [wq[:, ts(j, 256)], wk[:, ts(j, 256)], wv[:, ts(j, 256)]], axis=1)
        wqkv_j = np.ascontiguousarray(
            wqkv.reshape(NDC, 128, 768)).astype(np.float32)
        wb_j = np.ascontiguousarray(
            wb[:, j].reshape(NDC, 128, 1)).astype(np.float32)
        convw_j = np.stack([
            np.concatenate([cq[ts(j, 256)][ts(c, 128)],
                            ck[ts(j, 256)][ts(c, 128)],
                            cv[ts(j, 256)][ts(c, 128)]], axis=1)
            for c in range(2)]).astype(np.float32)
        in_maps.append({
            "x_loc": np.ascontiguousarray(xf[ts(j, NT)]),
            "wg_t": wg_t, "wu_t": wu_t, "wd_t": wd_t,
            "sg_in": sg_in, "su_in": su_in, "sd_in": sd_in,
            "wqkv_in": wqkv_j, "wb_in": wb_j, "convw_in": convw_j,
            "wo_in": wo_in, "nrmw_in": nrmw_in,
            "c_ident": ident, "c_identf": identf, "c_twoI": twoI,
            "c_slmask": slmask, "c_iumask": iumask,
            "c_ones1": ones1, "c_half1": half1,
            "ooff_in": np.array(
                [[(16384 * [0, 0, 2, 2, 1, 1, 3, 3][j] + 2048 * h
                   + 1024 * (j % 2)) * DV] for h in range(H)],
                dtype=np.int32),
            "c_onesc": np.ones((128, 1), dtype=bf),
            "c_ones1f": np.ones((1, 128), dtype=np.float32),
            "fconst_in": np.tile(np.array(
                [[LN_EPS, 256 * L2_EPS, RMS_EPS, 256.0, 1.0 / 256, 1.0, L2_EPS,
                  0.0]], dtype=np.float32), (128, 1)),
        })
    return in_maps


def kernel(**inputs):
    in_maps = _host_layout(inputs)
    if "nc" not in _NC_CACHE:
        _NC_CACHE["nc"] = build_nc()
    nc = _NC_CACHE["nc"]
    res = run_bass_kernel_spmd(nc, in_maps, core_ids=list(range(NCORE)))
    out = np.concatenate([res.results[j]["out_loc"] for j in range(NCORE)],
                         axis=0)
    return out.reshape(B, T, D)


# version 09
# speedup vs baseline: 1.0648x; 1.0648x over previous
import dataclasses
import os

import numpy as np
import ml_dtypes

import concourse.bass as bass
import concourse.mybir as mybir
from concourse.bass_utils import run_bass_kernel_spmd

F32 = mybir.dt.float32
BF16 = mybir.dt.bfloat16
I32 = mybir.dt.int32
I8 = mybir.dt.int8
AF = mybir.ActivationFunctionType
ALU = mybir.AluOpType
AX = mybir.AxisListType.X

NCORE = 8
B, T, D = 2, 4096, 2048
NT = 1024
H, DK, DV = 8, 256, 256
HID = 5632
NHC = HID // 128
NDC = D // 128
CH = 128
NCH = T // CH
GT = B * T
NW = 16
NG = 16
LN_EPS = 1e-5
L2_EPS = 1e-6
RMS_EPS = 1e-5

MK_DBG = os.environ.get("MK_DBG", "")
MK_PH = os.environ.get("MK_PH", "full")


class Ctr:
    def __init__(self, nc, name):
        self.h = nc.alloc_semaphore(name)
        self.v = 0


class Arena:
    def __init__(self, nc, name, nbytes):
        self.t = nc.alloc_sbuf_tensor(name, [128, nbytes], I8)
        self.nbytes = nbytes
        self.off = 0

    def reset(self, off=0):
        self.off = off

    def take(self, shape, dtype):
        esz = mybir.dt.size(dtype)
        start = (self.off + 31) // 32 * 32
        n = shape[1] * esz
        assert shape[0] <= 128
        assert start + n <= self.nbytes, (start, n, self.nbytes)
        self.off = start + n
        return self.t[0:shape[0], start:start + n].bitcast(dtype)


def ts(i, n):
    return slice(i * n, (i + 1) * n)


def build_nc():
    nc = bass.Bass("TRN2", target_bir_lowering=False)

    x_in = nc.dram_tensor("x_loc", [NT, D], F32, kind="ExternalInput").ap()
    wg_t = nc.dram_tensor("wg_t", [NHC, 128, D], I32, kind="ExternalInput").ap()
    wu_t = nc.dram_tensor("wu_t", [NHC, 128, D], I32, kind="ExternalInput").ap()
    wd_t = nc.dram_tensor("wd_t", [4, NHC, 128, 512], I32, kind="ExternalInput").ap()
    sg_in = nc.dram_tensor("sg_in", [128, NHC], F32, kind="ExternalInput").ap()
    su_in = nc.dram_tensor("su_in", [128, NHC], F32, kind="ExternalInput").ap()
    sd_in = nc.dram_tensor("sd_in", [1, D], F32, kind="ExternalInput").ap()
    wqkv_in = nc.dram_tensor("wqkv_in", [NDC, 128, 768], F32, kind="ExternalInput").ap()
    wb_in = nc.dram_tensor("wb_in", [NDC, 128, 1], F32, kind="ExternalInput").ap()
    convw_in = nc.dram_tensor("convw_in", [2, 128, 12], F32, kind="ExternalInput").ap()
    wo_in = nc.dram_tensor("wo_in", [D, D], F32, kind="ExternalInput").ap()
    nrmw_in = nc.dram_tensor("nrmw_in", [128, NDC], F32, kind="ExternalInput").ap()
    c_ident = nc.dram_tensor("c_ident", [128, 128], BF16, kind="ExternalInput").ap()
    c_identf = nc.dram_tensor("c_identf", [128, 128], F32, kind="ExternalInput").ap()
    c_twoI = nc.dram_tensor("c_twoI", [128, 512], BF16, kind="ExternalInput").ap()
    c_slmask = nc.dram_tensor("c_slmask", [128, 128], BF16, kind="ExternalInput").ap()
    c_iumask = nc.dram_tensor("c_iumask", [128, 128], BF16, kind="ExternalInput").ap()
    c_ones1 = nc.dram_tensor("c_ones1", [1, 128], BF16, kind="ExternalInput").ap()
    c_half1 = nc.dram_tensor("c_half1", [1, 128], BF16, kind="ExternalInput").ap()
    c_onesc = nc.dram_tensor("c_onesc", [128, 1], BF16, kind="ExternalInput").ap()
    c_ones1f = nc.dram_tensor("c_ones1f", [1, 128], F32, kind="ExternalInput").ap()
    ooff_in = nc.dram_tensor("ooff_in", [16, 1], I32, kind="ExternalInput").ap()
    fconst_in = nc.dram_tensor("fconst_in", [128, 8], F32, kind="ExternalInput").ap()

    out_d = nc.dram_tensor("out_loc", [NT, D], F32, kind="ExternalOutput").ap()
    dbg_d = nc.dram_tensor("dbg", [NT, D], F32, kind="ExternalOutput").ap() \
        if MK_DBG else None

    y0b = nc.dram_tensor("y0b", [NT, D], BF16).ap()
    x1b = nc.dram_tensor("x1b", [NT, D], F32).ap()
    y1c = nc.dram_tensor("y1c", [NT, D], BF16).ap()
    gy0 = nc.dram_tensor("gy0", [NCORE * 512, D], BF16).ap()
    gy1 = nc.dram_tensor("gy1", [NCORE * 512, D], BF16).ap()
    q_s = nc.dram_tensor("q_s", [DK, GT], BF16).ap()
    k_s = nc.dram_tensor("k_s", [DK, GT], BF16).ap()
    v_s = nc.dram_tensor("v_s", [DV, GT], BF16).ap()
    o_c = nc.dram_tensor("o_c", [GT, DV], BF16).ap()
    go = nc.dram_tensor("go", [NCORE * GT, DV], BF16).ap()
    o_m = nc.dram_tensor("o_m", [H * NT, DV], BF16).ap()
    x2b = nc.dram_tensor("x2b", [NT, D], F32).ap()
    y2b = nc.dram_tensor("y2b", [NT, D], BF16).ap()
    wg_bf = nc.dram_tensor("wg_bf", [NHC, 128, D], BF16).ap()
    wu_bf = nc.dram_tensor("wu_bf", [NHC, 128, D], BF16).ap()
    wd_bf = nc.dram_tensor("wd_bf", [4, NHC, 128, 512], BF16).ap()
    beta_d = nc.dram_tensor("beta_d", [1, GT], F32).ap()

    rg = [list(range(NCORE))]

    tarena = nc.alloc_sbuf_tensor("tarena", [128, NDC * 1024], BF16,
                                  is_transpose_dest=True)
    hid = nc.alloc_sbuf_tensor("hid", [128, NHC * 1024], BF16)
    ar = Arena(nc, "arena", 64 * 1024)
    sgu_sb = nc.alloc_sbuf_tensor("sgu_sb", [128, 2 * NHC], F32)
    sdb_sb = nc.alloc_sbuf_tensor("sdb_sb", [128, D], F32)
    consts = nc.alloc_sbuf_tensor("consts", [128, 1024], BF16)
    identf_sb = nc.alloc_sbuf_tensor("identf_sb", [128, 128], F32)
    ones_sb = nc.alloc_sbuf_tensor("ones_sb", [1, 256], BF16)
    convw_sb = nc.alloc_sbuf_tensor("convw_sb", [128, 24], F32)
    cdiag = nc.alloc_sbuf_tensor("cdiag", [128, 24 * 128], BF16)
    wb_sb = nc.alloc_sbuf_tensor("wb_sb", [128, NDC], BF16)
    nrmw_sb = nc.alloc_sbuf_tensor("nrmw_sb", [128, NDC], F32)
    ooff_sb = nc.alloc_sbuf_tensor("ooff_sb", [16, 1], I32)
    onesc_sb = nc.alloc_sbuf_tensor("onesc_sb", [128, 1], BF16)
    ones1f_sb = nc.alloc_sbuf_tensor("ones1f_sb", [1, 128], F32)
    fc_sb = nc.alloc_sbuf_tensor("fc_sb", [128, 8], F32)

    ident = consts[:, 0:128]
    twoI = consts[:, 128:640]
    slm = consts[:, 640:768]
    ium = consts[:, 768:896]
    ones1 = ones_sb[0:1, 0:128]
    half1 = ones_sb[0:1, 128:256]

    pb = [nc.alloc_psum_tensor(f"pb{i}", [128, 512], F32) for i in range(8)]

    cP = Ctr(nc, "cP")
    cV = Ctr(nc, "cV")
    cA = Ctr(nc, "cA")
    cO = Ctr(nc, "cO")
    wl = [Ctr(nc, f"wl{i}") for i in range(4)]
    xl = [Ctr(nc, f"xl{i}") for i in range(4)]
    sB = [Ctr(nc, f"sB{i}") for i in range(2)]
    sOne = Ctr(nc, "sOne")
    st_y0 = Ctr(nc, "st_y0")
    st_x1 = Ctr(nc, "st_x1")
    st_y1 = Ctr(nc, "st_y1")
    st_qkv = Ctr(nc, "st_qkv")
    st_o = Ctr(nc, "st_o")
    st_oh = Ctr(nc, "st_oh")
    st_y1h = Ctr(nc, "st_y1h")
    st_x2 = Ctr(nc, "st_x2")
    st_y2 = Ctr(nc, "st_y2")
    st_out = Ctr(nc, "st_out")
    svV = Ctr(nc, "svV")
    svA = Ctr(nc, "svA")
    svO = Ctr(nc, "svO")
    st_cgu = Ctr(nc, "st_cgu")
    st_cd = Ctr(nc, "st_cd")
    sC = Ctr(nc, "sC")

    def SY(e, c, ins):
        c.v += 1
        ins.then_inc(c.h, 1)
        e.wait_ge(c.h, c.v)

    def FIN(e, sv, c, ins, k=1):
        SY(e, sv, ins)
        c.v += k
        e.nop().then_inc(c.h, k)

    def FND(e, c, ins, k=1):
        c.v += k
        ins.then_inc(c.h, k)

    st_of = {"y0b": st_y0, "x1b": st_x1, "y1c": st_y1, "x2b": st_x2,
             "y2b": st_y2, "out": st_out}

    with nc.Block() as blk:
        ar.reset()
        sd_row = ar.take([1, D], F32)
        sd_bf = ar.take([1, D], BF16)
        b_one = sOne.v

        @blk.sync
        def _(e):
            for dst, src in [
                (consts[:, 0:128], c_ident), (consts[:, 128:640], c_twoI),
                (consts[:, 640:768], c_slmask), (consts[:, 768:896], c_iumask),
                (identf_sb[:, :], c_identf),
                (ones_sb[0:1, 0:128], c_ones1), (ones_sb[0:1, 128:256], c_half1),
                (sgu_sb[:, 0:NHC], sg_in), (sgu_sb[:, NHC:2 * NHC], su_in),
                (convw_sb[:, 0:12], convw_in[0]), (convw_sb[:, 12:24], convw_in[1]),
                (nrmw_sb[:, :], nrmw_in),
                (ooff_sb[:], ooff_in), (sd_row, sd_in), (fc_sb[:], fconst_in),
                (onesc_sb[:], c_onesc), (ones1f_sb[:], c_ones1f),
            ]:
                e.dma_start(dst, src).then_inc(sOne.h, 16)
                sOne.v += 16

        @blk.tensor
        def _(e):
            e.wait_ge(cV.h, cV.v + 1)
            for i in range(4):
                ins = e.matmul(pb[i][:, 0:512], half1, sd_bf[0:1, ts(i, 512)],
                               start=True, stop=True)
            ins.then_inc(cP.h, 1)
            cP.v += 1

        @blk.vector
        def _(e):
            e.wait_ge(sOne.h, b_one + 17 * 16)
            ins = e.tensor_copy(sd_bf, sd_row)
            FIN(e, svV, cV, ins)
            e.wait_ge(cP.h, cP.v)
            for i in range(4):
                ins = e.tensor_copy(sdb_sb[:, ts(i, 512)], pb[i][:, 0:512])
            FIN(e, svV, cV, ins)
            for m_ in range(3):
                for c_ in range(2):
                    for s_ in range(4):
                        j_ = (2 * m_ + c_) * 4 + s_
                        wc = convw_sb[:, 12 * c_ + 4 * m_ + (3 - s_):
                                      12 * c_ + 4 * m_ + (4 - s_)]
                        ins = e.tensor_scalar(cdiag[:, ts(j_, 128)], ident,
                                              wc, None, ALU.mult)
            FIN(e, svV, cV, ins)

        @blk.scalar
        def _(e):
            pass

        @blk.gpsimd
        def _(e):
            pass

    def emit_ln(x_src, y_dst, in_waits, st_out_c, half_ctr=None, prog=None):
        with nc.Block() as blk:
            ar.reset()
            xs = [ar.take([128, D], F32) for _ in range(2)]
            scr = ar.take([128, D], F32)
            stat = ar.take([128, 16], F32)
            yt = [ar.take([128, D], BF16) for _ in range(2)]
            b_v, b_c = cV.v, cA.v
            iw = [(c.h, c.v) for c in (in_waits or [])]
            marks = []

            @blk.scalar
            def _(e):
                for hh_, vv_ in iw:
                    e.wait_ge(hh_, vv_)
                for t in range(8):
                    if t == 0:
                        if prog is not None:
                            e.wait_ge(*prog[0])
                        e.dma_start(xs[0], x_src[ts(0, 128), :]).then_inc(
                            xl[0].h, 16)
                        xl[0].v += 16
                    if t + 1 < 8:
                        if t >= 1:
                            e.wait_ge(cV.h, b_v + 2 * (t - 1) + 2)
                        if prog is not None:
                            e.wait_ge(*prog[t + 1])
                        e.dma_start(xs[(t + 1) % 2],
                                    x_src[ts(t + 1, 128), :]).then_inc(
                                        xl[(t + 1) % 2].h, 16)
                        xl[(t + 1) % 2].v += 16
                    e.wait_ge(xl[t % 2].h, xl[t % 2].v)
                    if t >= 1:
                        e.wait_ge(cV.h, b_v + 2 * (t - 1) + 1)
                    ins = e.activation(scr, xs[t % 2], AF.Square,
                                       accum_out=stat[:, 1:2])
                    FIN(e, svA, cA, ins)
                    e.wait_ge(cV.h, b_v + 2 * t + 1)
                    ins = e.activation(stat[:, 6:7], stat[:, 5:6], AF.Sqrt,
                                       bias=fc_sb[:, 0:1])
                    FIN(e, svA, cA, ins)
                    e.wait_ge(cV.h, b_v + 2 * t + 2)
                    ctr = half_ctr if (half_ctr is not None and t < 4) \
                        else st_out_c
                    e.dma_start(y_dst[ts(t, 128), :], yt[t % 2]).then_inc(
                        ctr.h, 16)
                    ctr.v += 16
                    marks.append((ctr.h, ctr.v))

            @blk.vector
            def _(e):
                for t in range(8):
                    e.wait_ge(cA.h, b_c + 2 * t + 1)
                    e.reduce_sum(stat[:, 0:1], xs[t % 2], axis=AX)
                    ins = e.tensor_scalar(stat[:, 4:5], stat[:, 1:2], 1.0 / D,
                                          None, ALU.mult)
                    SY(e, svV, ins)
                    ins = e.tensor_scalar(stat[:, 2:3], stat[:, 0:1], 1.0 / D,
                                          None, ALU.mult)
                    SY(e, svV, ins)
                    ins = e.tensor_tensor(stat[:, 3:4], stat[:, 2:3],
                                          stat[:, 2:3], ALU.mult)
                    SY(e, svV, ins)
                    ins = e.tensor_tensor(stat[:, 5:6], stat[:, 4:5],
                                          stat[:, 3:4], ALU.subtract)
                    FND(e, cV, ins)
                    e.wait_ge(cA.h, b_c + 2 * t + 2)
                    ins = e.reciprocal(stat[:, 7:8], stat[:, 6:7])
                    SY(e, svV, ins)
                    if t >= 2:
                        e.wait_ge(*marks[t - 1])
                    ins = e.tensor_scalar(yt[t % 2], xs[t % 2], stat[:, 2:3],
                                          stat[:, 7:8], ALU.subtract, ALU.mult)
                    FND(e, cV, ins)

            @blk.sync
            def _(e):
                pass

            @blk.tensor
            def _(e):
                pass

            @blk.gpsimd
            def _(e):
                pass

    def emit_gateup(y_dram, st_y, use_int, cache_bf):
        with nc.Block() as blk:
            ar.reset()
            if use_int:
                wi_g = [ar.take([128, D], I32) for _ in range(2)]
                wi_u = [ar.take([128, D], I32) for _ in range(2)]
            wbg = [ar.take([128, D], BF16) for _ in range(2)]
            wbu = [ar.take([128, D], BF16) for _ in range(2)]
            tg = [ar.take([128, 1024], BF16) for _ in range(2)]
            yT = tarena
            b_p, b_v, b_a, b_o = cP.v, cV.v, cA.v, cO.v
            b_sb = sB[0].v
            b_cgu = st_cgu.v
            b_wl = [w.v for w in wl]
            w_y = st_y.v
            w_cgu_prev = st_cgu.v

            @blk.sync
            def _(e):
                e.wait_ge(st_y.h, w_y)
                if not use_int:
                    e.wait_ge(st_cgu.h, w_cgu_prev)
                for k in range(NDC):
                    e.dma_start(yT[:, ts(k, 1024)], y_dram[:, ts(k, 128)],
                                transpose=True).then_inc(sB[0].h, 16)
                sB[0].v += 16 * NDC
                for h in range(NHC):
                    if use_int:
                        if h >= 2:
                            e.wait_ge(cV.h, b_v + 2 * (h - 2) + 1)
                            e.wait_ge(cO.h, b_o + (h - 2) + 1)
                        e.dma_start(wi_g[h % 2], wg_t[h]).then_inc(
                            wl[h % 2].h, 16)
                        wl[h % 2].v += 16
                        e.dma_start(wi_u[h % 2], wu_t[h]).then_inc(
                            wl[2 + h % 2].h, 16)
                        wl[2 + h % 2].v += 16
                    else:
                        if h >= 2:
                            e.wait_ge(cP.h, b_p + (h - 2) + 1)
                        e.dma_start(wbg[h % 2], wg_bf[h]).then_inc(
                            wl[h % 2].h, 16)
                        wl[h % 2].v += 16
                        e.dma_start(wbu[h % 2], wu_bf[h]).then_inc(
                            wl[2 + h % 2].h, 16)
                        wl[2 + h % 2].v += 16

            @blk.vector
            def _(e):
                for h in range(NHC + 1):
                    if use_int and h < NHC:
                        e.wait_ge(wl[h % 2].h, b_wl[h % 2] + 16 * (h // 2 + 1))
                        if h >= 2:
                            e.wait_ge(cP.h, b_p + (h - 2) + 1)
                            if cache_bf:
                                e.wait_ge(st_cgu.h, b_cgu + 32 * h)
                        ins = e.tensor_scalar(wbg[h % 2], wi_g[h % 2], -128.0,
                                              None, ALU.add)
                        FIN(e, svV, cV, ins)
                    else:
                        e.nop().then_inc(cV.h, 1)
                        cV.v += 1
                    if h >= 1:
                        hh = h - 1
                        e.wait_ge(cA.h, b_a + 2 * (hh + 1))
                        ins = e.tensor_tensor(hid[:, ts(hh, 1024)],
                                              hid[:, ts(hh, 1024)],
                                              tg[hh % 2], ALU.mult)
                        FIN(e, svV, cV, ins)
                    else:
                        e.nop().then_inc(cV.h, 1)
                        cV.v += 1

            @blk.gpsimd
            def _(e):
                if use_int and cache_bf:
                    for h in range(NHC):
                        e.wait_ge(cV.h, b_v + 2 * h + 1)
                        e.wait_ge(cO.h, b_o + h + 1)
                        e.dma_start(wg_bf[h], wbg[h % 2]).then_inc(st_cgu.h, 16)
                        e.dma_start(wu_bf[h], wbu[h % 2]).then_inc(st_cgu.h, 16)
                        st_cgu.v += 32

            @blk.tensor
            def _(e):
                e.wait_ge(sB[0].h, b_sb + 16 * NDC)
                for h in range(NHC):
                    if use_int:
                        e.wait_ge(cV.h, b_v + 2 * h + 1)
                        e.wait_ge(cO.h, b_o + h + 1)
                        if cache_bf and h >= 2:
                            e.wait_ge(st_cgu.h, b_cgu + 32 * h)
                    else:
                        e.wait_ge(wl[h % 2].h, b_wl[h % 2] + 16 * (h // 2 + 1))
                        e.wait_ge(wl[2 + h % 2].h,
                                  b_wl[2 + h % 2] + 16 * (h // 2 + 1))
                    if h >= 2:
                        e.wait_ge(cA.h, b_a + 2 * (h - 1))
                    pg = (h % 2) * 2
                    pu = 4 + (h % 2) * 2
                    for k in range(NDC):
                        st_, sp = (k == 0), (k == NDC - 1)
                        wgs = wbg[h % 2][:, ts(k, 128)]
                        wus = wbu[h % 2][:, ts(k, 128)]
                        e.matmul(pb[pg][:, 0:512], wgs,
                                 yT[:, k * 1024:k * 1024 + 512],
                                 start=st_, stop=sp)
                        e.matmul(pb[pg + 1][:, 0:512], wgs,
                                 yT[:, k * 1024 + 512:(k + 1) * 1024],
                                 start=st_, stop=sp)
                        e.matmul(pb[pu][:, 0:512], wus,
                                 yT[:, k * 1024:k * 1024 + 512],
                                 start=st_, stop=sp)
                        ins = e.matmul(pb[pu + 1][:, 0:512], wus,
                                       yT[:, k * 1024 + 512:(k + 1) * 1024],
                                       start=st_, stop=sp)
                    ins.then_inc(cP.h, 1)
                    cP.v += 1

            @blk.scalar
            def _(e):
                for h in range(NHC + 1):
                    if use_int and h < NHC:
                        e.wait_ge(wl[2 + h % 2].h,
                                  b_wl[2 + h % 2] + 16 * (h // 2 + 1))
                        if h >= 2:
                            e.wait_ge(cP.h, b_p + (h - 2) + 1)
                            if cache_bf:
                                e.wait_ge(st_cgu.h, b_cgu + 32 * h)
                        ins = e.activation(wbu[h % 2], wi_u[h % 2], AF.Copy,
                                           bias=-128.0)
                        FIN(e, svA, cO, ins)
                    if h == 0:
                        continue
                    hh = h - 1
                    e.wait_ge(cP.h, b_p + hh + 1)
                    if hh >= 2:
                        e.wait_ge(cV.h, b_v + 2 * hh)
                    pg = (hh % 2) * 2
                    pu = 4 + (hh % 2) * 2
                    sgs = sgu_sb[:, hh:hh + 1]
                    sus = sgu_sb[:, NHC + hh:NHC + hh + 1]
                    e.activation(tg[hh % 2][:, 0:512], pb[pg][:, 0:512], AF.Silu,
                                 scale=sgs)
                    e.activation(tg[hh % 2][:, 512:1024], pb[pg + 1][:, 0:512],
                                 AF.Silu, scale=sgs)
                    e.activation(hid[:, hh * 1024:hh * 1024 + 512],
                                 pb[pu][:, 0:512], AF.Copy, scale=sus)
                    ins = e.activation(hid[:, hh * 1024 + 512:(hh + 1) * 1024],
                                       pb[pu + 1][:, 0:512], AF.Copy, scale=sus)
                    FIN(e, svA, cA, ins, 2)

    def emit_down(x_src, x1_dst, in_waits, use_int, cache_bf,
                  final=False, tag="a"):
        NKG = NHC // 4
        with nc.Block() as blk:
            ar.reset()
            if use_int:
                wdi = [ar.take([128, 2048], I32) for _ in range(2)]
            wdb = [ar.take([128, 2048], BF16) for _ in range(4)]
            xsl = [ar.take([128, 512], F32) for _ in range(4)]
            m1 = [ar.take([128, 512], F32) for _ in range(2)]
            x1sl = [ar.take([128, 512], F32) for _ in range(2)]
            b_p, b_v, b_a, b_cd = cP.v, cV.v, cA.v, st_cd.v
            b_wl = [w.v for w in wl]
            dxl = [Ctr(nc, f"dxl_{tag}{i}") for i in range(4)]
            stsl = [Ctr(nc, f"stsl_{tag}{i}") for i in range(2)]
            b_xl = [w.v for w in dxl]
            w_cgu_now = st_cgu.v
            iw = [(c.h, c.v) for c in (in_waits or [])]
            w_cd_prev = st_cd.v

            def wd_ap(dram, dq, g):
                return dram[dq, ts(g, 4)].transpose([1, 0, 2])

            def wd_sb3(t):
                return t.rearrange("p (g q) -> p g q", g=4)

            @blk.sync
            def _(e):
                e.wait_ge(st_cgu.h, w_cgu_now)
                if not use_int:
                    e.wait_ge(st_cd.h, w_cd_prev)
                for i in range(4 * NKG):
                    dq, g = divmod(i, NKG)
                    if use_int:
                        if i >= 2:
                            e.wait_ge(cA.h, b_a + (i - 2) + 1)
                        e.dma_start(wd_sb3(wdi[i % 2]),
                                    wd_ap(wd_t, dq, g)).then_inc(
                            wl[i % 2].h, 16)
                        wl[i % 2].v += 16
                    else:
                        if i >= 4:
                            e.wait_ge(cP.h, b_p + (i - 4) + 1)
                        e.dma_start(wd_sb3(wdb[i % 4]),
                                    wd_ap(wd_bf, dq, g)).then_inc(
                            wl[i % 4].h, 16)
                        wl[i % 4].v += 16

            @blk.scalar
            def _(e):
                if use_int:
                    for i in range(4 * NKG):
                        e.wait_ge(wl[i % 2].h, b_wl[i % 2] + 16 * (i // 2 + 1))
                        if i >= 4:
                            e.wait_ge(cP.h, b_p + (i - 4) + 1)
                        if cache_bf and i >= 4:
                            e.wait_ge(st_cd.h, b_cd + 16 * (i - 4 + 1))
                        ins = e.activation(wdb[i % 4], wdi[i % 2], AF.Copy,
                                           bias=-128.0)
                        FIN(e, svA, cA, ins)

            @blk.gpsimd
            def _(e):
                e.wait_ge(st_cgu.h, w_cgu_now)
                for hh_, vv_ in iw:
                    e.wait_ge(hh_, vv_)
                for dq in range(4):
                    if use_int and cache_bf:
                        for g in range(NKG):
                            i = dq * NKG + g
                            e.wait_ge(cA.h, b_a + i + 1)
                            e.dma_start(wd_ap(wd_bf, dq, g),
                                        wd_sb3(wdb[i % 4])).then_inc(st_cd.h, 16)
                            st_cd.v += 16
                    for t in range(8):
                        j = dq * 8 + t
                        if j >= 4:
                            e.wait_ge(cV.h, b_v + 2 * (j - 4) + 2)
                        e.dma_start(xsl[j % 4],
                                    x_src[ts(t, 128), ts(dq, 512)]).then_inc(
                                        dxl[j % 4].h, 16)
                        dxl[j % 4].v += 16
                        if j >= 2:
                            jj = j - 2
                            e.wait_ge(cV.h, b_v + 2 * jj + 2)
                            tq, tt_ = divmod(jj, 8)
                            e.dma_start(x1_dst[ts(tt_, 128), ts(tq, 512)],
                                        x1sl[jj % 2]).then_inc(stsl[jj % 2].h, 16)
                            stsl[jj % 2].v += 16
                for jj in [30, 31]:
                    e.wait_ge(cV.h, b_v + 2 * jj + 2)
                    tq, tt_ = divmod(jj, 8)
                    e.dma_start(x1_dst[ts(tt_, 128), ts(tq, 512)],
                                x1sl[jj % 2]).then_inc(stsl[jj % 2].h, 16)
                    stsl[jj % 2].v += 16
                if final:
                    for s_ in stsl:
                        e.wait_ge(s_.h, s_.v)

            @blk.tensor
            def _(e):
                for i in range(4 * NKG):
                    dq, g = divmod(i, NKG)
                    if use_int:
                        e.wait_ge(cA.h, b_a + i + 1)
                    else:
                        e.wait_ge(wl[i % 4].h, b_wl[i % 4] + 16 * (i // 4 + 1))
                    if g == 0 and dq >= 1:
                        e.wait_ge(cV.h, b_v + 16 * dq)
                    for kk in range(4):
                        k = g * 4 + kk
                        st_, sp = (k == 0), (k == NHC - 1)
                        for t in range(8):
                            ins = e.matmul(
                                pb[t][:, 0:512],
                                hid[:, k * 1024 + t * 128:k * 1024 + (t + 1) * 128],
                                wdb[i % 4][:, ts(kk, 512)], start=st_, stop=sp)
                    ins.then_inc(cP.h, 1)
                    cP.v += 1

            @blk.vector
            def _(e):
                for dq in range(4):
                    e.wait_ge(cP.h, b_p + NKG * (dq + 1))
                    for t in range(8):
                        j = dq * 8 + t
                        e.wait_ge(dxl[j % 4].h, b_xl[j % 4] + 16 * (j // 4 + 1))
                        if j >= 2:
                            e.wait_ge(stsl[j % 2].h, 16 * ((j - 2) // 2 + 1))
                        ins = e.tensor_tensor(m1[j % 2], pb[t][:, 0:512],
                                              sdb_sb[:, ts(dq, 512)], ALU.mult)
                        FIN(e, svV, cV, ins)
                        ins = e.tensor_tensor(x1sl[j % 2], m1[j % 2],
                                              xsl[j % 4], ALU.add)
                        FND(e, cV, ins)
        marks = {t: (stsl[t % 2].h, 16 * ((24 + t) // 2 + 1))
                 for t in range(8)}
        return stsl, marks

    if MK_PH != "p0":
        emit_ln(x_in, y0b, None, st_y0)
    if MK_PH not in ("p0", "ln"):
        emit_gateup(y0b, st_y0, use_int=True, cache_bf=True)
    if MK_PH not in ("p0", "ln", "gu"):
        x1_st, x1_marks = emit_down(x_in, x1b, None, use_int=True,
                                    cache_bf=False, tag='p1')
    if MK_PH not in ("p0", "ln", "gu", "dn"):
        emit_ln(x1b, y1c, None, st_y1, half_ctr=st_y1h, prog=x1_marks)

    if MK_PH in ("p0", "ln", "gu"):
        with nc.Block() as blk:
            @blk.gpsimd
            def _(e):
                e.dma_start(out_d, x_in).then_inc(sC.h, 16)
                sC.v += 16
                e.wait_ge(sC.h, sC.v)

            @blk.sync
            def _(e):
                pass

            @blk.scalar
            def _(e):
                pass

            @blk.vector
            def _(e):
                pass

            @blk.tensor
            def _(e):
                pass
        return nc

    if MK_PH in ("p1", "dn"):
        cpsrc = os.environ.get("MK_CP", "x1")
        with nc.Block() as blk:
            x1_waits = [(c.h, c.v) for c in x1_st]
            sStub = Ctr(nc, "sStub")

            @blk.gpsimd
            def _(e):
                for hh_, vv_ in x1_waits:
                    e.wait_ge(hh_, vv_)
                ob = out_d.bitcast(BF16)
                if cpsrc == "y0":
                    pairs = [(ob[:, 0:D], y0b)]
                elif cpsrc == "yt":
                    pairs = [(ob[0:128, :], tarena[:, 0:4096]),
                             (ob[128:256, :], tarena[:, 4096:8192])]
                elif cpsrc == "hid":
                    pairs = [(ob[0:128, :], hid[:, 0:4096]),
                             (ob[128:256, :], hid[:, 4096:8192]),
                             (ob[256:384, :], hid[:, 40960:45056])]
                else:
                    pairs = [(out_d, x1b)]
                for dst, srcp in pairs:
                    e.dma_start(dst, srcp).then_inc(sStub.h, 16)
                    sStub.v += 16
                e.wait_ge(sStub.h, sStub.v)

            @blk.sync
            def _(e):
                pass

            @blk.scalar
            def _(e):
                pass

            @blk.vector
            def _(e):
                pass

            @blk.tensor
            def _(e):
                pass
        return nc

    with nc.Block() as blk:
        w_y1 = st_y1.v
        w_y1h = st_y1h.v

        @blk.gpsimd
        def _(e):
            e.wait_ge(st_y1h.h, w_y1h)
            e.collective_compute("AllGather", ALU.bypass, rg,
                                 [y1c[0:512, :].opt()],
                                 [gy0.opt()]).then_inc(sC.h)
            sC.v += 1
            e.wait_ge(st_y1.h, w_y1)
            e.collective_compute("AllGather", ALU.bypass, rg,
                                 [y1c[512:1024, :].opt()],
                                 [gy1.opt()]).then_inc(sC.h)
            sC.v += 1

        @blk.sync
        def _(e):
            pass

        @blk.scalar
        def _(e):
            pass

        @blk.vector
        def _(e):
            pass

        @blk.tensor
        def _(e):
            pass

    def emit_proj():
        wqkv_sb = hid[:, 0:NDC * 768]
        with nc.Block() as blk:
            ar.reset()
            wqf = [ar.take([128, 768], F32) for _ in range(2)]
            wbf = [ar.take([128, 1], F32) for _ in range(2)]
            b_wl = [w.v for w in wl]
            b_a = cA.v

            @blk.sync
            def _(e):
                for k in range(NDC):
                    if k >= 2:
                        e.wait_ge(cA.h, b_a + 2 * (k - 2) + 2)
                    e.dma_start(wqf[k % 2], wqkv_in[k]).then_inc(wl[k % 2].h, 16)
                    wl[k % 2].v += 16
                    e.dma_start(wbf[k % 2], wb_in[k]).then_inc(wl[2 + k % 2].h, 16)
                    wl[2 + k % 2].v += 16

            @blk.scalar
            def _(e):
                for k in range(NDC):
                    e.wait_ge(wl[k % 2].h, b_wl[k % 2] + 16 * (k // 2 + 1))
                    e.wait_ge(wl[2 + k % 2].h, b_wl[2 + k % 2] + 16 * (k // 2 + 1))
                    e.activation(wqkv_sb[:, ts(k, 768)], wqf[k % 2], AF.Copy)
                    ins = e.activation(wb_sb[:, k:k + 1], wbf[k % 2], AF.Copy)
                    FIN(e, svA, cA, ins, 2)

            @blk.vector
            def _(e):
                pass

            @blk.tensor
            def _(e):
                pass

            @blk.gpsimd
            def _(e):
                pass

        with nc.Block() as blk:
            ar.reset()
            qraw = [[ar.take([128, 512], BF16) for _ in range(6)] for _ in range(2)]
            sil = [ar.take([128, 512], BF16) for _ in range(6)]
            sqb = [ar.take([128, 512], BF16) for _ in range(2)]
            qn_sb = [ar.take([128, 512], BF16) for _ in range(2)]
            kn_sb = [ar.take([128, 512], BF16) for _ in range(2)]
            rrow = ar.take([1, 512], F32)
            rinv = ar.take([1, 512], F32)
            betw = ar.take([1, 512], F32)
            b_p, b_v, b_a = cP.v, cV.v, cA.v
            b_sB = [s.v for s in sB]
            b_st = st_qkv.v
            w_coll = sC.v

            PW, VW, AW = 12, 10, 8

            @blk.sync
            def _(e):
                for w in range(NW):
                    if w == 0:
                        e.wait_ge(sC.h, w_coll - 1)
                    if w == 1:
                        e.wait_ge(sC.h, w_coll)
                    if w >= 2:
                        e.wait_ge(cP.h, b_p + PW * (w - 2) + 2)
                    src_g = gy0 if w % 2 == 0 else gy1
                    for k in range(NDC):
                        e.dma_start(
                            tarena[:, (w % 2) * 8192 + 512 * k:
                                   (w % 2) * 8192 + 512 * (k + 1)],
                            src_g[ts(w // 2, 512), ts(k, 128)],
                            transpose=True).then_inc(sB[w % 2].h, 16)
                    sB[w % 2].v += 256

            @blk.tensor
            def _(e):
                e.wait_ge(cA.h, b_a)
                for w in range(NW):
                    e.wait_ge(sB[w % 2].h, b_sB[w % 2] + 256 * (w // 2 + 1))
                    if w >= 1:
                        e.wait_ge(cA.h, b_a + AW * (w - 1) + 6)
                    slab = tarena[:, (w % 2) * 8192:(w % 2) * 8192 + 8192]
                    for k in range(NDC):
                        st_, sp = (k == 0), (k == NDC - 1)
                        rhs = slab[:, ts(k, 512)]
                        for et in range(6):
                            ins = e.matmul(pb[et][:, 0:512],
                                           wqkv_sb[:, 768 * k + 128 * et:
                                                   768 * k + 128 * (et + 1)],
                                           rhs, start=st_, stop=sp)
                    ins.then_inc(cP.h, 1)
                    cP.v += 1
                    if w >= 1:
                        e.wait_ge(cV.h, b_v + VW * w)
                    for k in range(NDC):
                        ins = e.matmul(pb[6][0:1, 0:512],
                                       wb_sb[:, k:k + 1], slab[:, ts(k, 512)],
                                       start=(k == 0), stop=(k == NDC - 1))
                    ins.then_inc(cP.h, 1)
                    cP.v += 1
                    for b in range(6):
                        e.wait_ge(cV.h, b_v + VW * w + b + 1)
                        raw = qraw[w % 2][b]
                        prv = qraw[(w - 1) % 2][b]
                        for s in range(4):
                            dg = cdiag[:, ts(4 * b + s, 128)]
                            if s >= 1 and w % 8 != 0:
                                e.matmul(pb[b][:, 0:s], dg,
                                         prv[:, 512 - s:512],
                                         start=False, stop=False)
                            ins = e.matmul(pb[b][:, s:512], dg,
                                           raw[:, 0:512 - s],
                                           start=(s == 0), stop=(s == 3))
                        ins.then_inc(cP.h, 1)
                        cP.v += 1
                    e.wait_ge(cA.h, b_a + AW * w + 3)
                    e.matmul(pb[7][0:1, 0:512], onesc_sb[:, 0:1], sqb[0],
                             start=True, stop=False)
                    ins = e.matmul(pb[7][0:1, 0:512], onesc_sb[:, 0:1], sqb[1],
                                   start=False, stop=True)
                    ins.then_inc(cP.h, 1)
                    cP.v += 1
                    e.wait_ge(cV.h, b_v + VW * w + 7)
                    ins = e.matmul(pb[6][:, 0:512], ones1f_sb[0:1, :],
                                   rinv[0:1, :], start=True, stop=True)
                    ins.then_inc(cP.h, 1)
                    cP.v += 1
                    e.wait_ge(cA.h, b_a + AW * w + 7)
                    e.matmul(pb[7][0:1, 0:512], onesc_sb[:, 0:1], sqb[0],
                             start=True, stop=False)
                    ins = e.matmul(pb[7][0:1, 0:512], onesc_sb[:, 0:1], sqb[1],
                                   start=False, stop=True)
                    ins.then_inc(cP.h, 1)
                    cP.v += 1
                    e.wait_ge(cV.h, b_v + VW * w + 9)
                    ins = e.matmul(pb[6][:, 0:512], ones1f_sb[0:1, :],
                                   rinv[0:1, :], start=True, stop=True)
                    ins.then_inc(cP.h, 1)
                    cP.v += 1

            @blk.vector
            def _(e):
                for w in range(NW):
                    if w >= 1:
                        e.wait_ge(st_qkv.h, b_st + 112 * w)
                    e.wait_ge(cP.h, b_p + PW * w + 1)
                    for b in range(6):
                        ins = e.tensor_copy(qraw[w % 2][b], pb[b][:, 0:512])
                        FND(e, cV, ins)
                    e.wait_ge(cA.h, b_a + AW * w + 7)
                    ins = e.reciprocal(rinv[0:1, :], rrow[0:1, :])
                    FND(e, cV, ins)
                    e.wait_ge(cP.h, b_p + PW * w + 10)
                    e.tensor_tensor(qn_sb[0], sil[0], pb[6][:, 0:512], ALU.mult)
                    ins = e.tensor_tensor(qn_sb[1], sil[1], pb[6][:, 0:512],
                                          ALU.mult)
                    FND(e, cV, ins)
                    e.wait_ge(cA.h, b_a + AW * w + 8)
                    ins = e.reciprocal(rinv[0:1, :], rrow[0:1, :])
                    FND(e, cV, ins)
                    e.wait_ge(cP.h, b_p + PW * w + 12)
                    e.tensor_tensor(kn_sb[0], sil[2], pb[6][:, 0:512], ALU.mult)
                    ins = e.tensor_tensor(kn_sb[1], sil[3], pb[6][:, 0:512],
                                          ALU.mult)
                    FND(e, cV, ins)

            @blk.scalar
            def _(e):
                for w in range(NW):
                    if w >= 1:
                        e.wait_ge(st_qkv.h, b_st + 112 * w)
                    e.wait_ge(cP.h, b_p + PW * w + 2)
                    ins = e.activation(betw[0:1, :], pb[6][0:1, 0:512],
                                       AF.Sigmoid)
                    FIN(e, svA, cA, ins)
                    e.wait_ge(cP.h, b_p + PW * w + 4)
                    e.activation(sil[0], pb[0][:, 0:512], AF.Silu)
                    ins = e.activation(sil[1], pb[1][:, 0:512], AF.Silu)
                    FIN(e, svA, cA, ins)
                    e.activation(sqb[0], sil[0], AF.Square)
                    ins = e.activation(sqb[1], sil[1], AF.Square)
                    FIN(e, svA, cA, ins)
                    e.wait_ge(cP.h, b_p + PW * w + 6)
                    e.activation(sil[2], pb[2][:, 0:512], AF.Silu)
                    ins = e.activation(sil[3], pb[3][:, 0:512], AF.Silu)
                    FIN(e, svA, cA, ins)
                    e.wait_ge(cP.h, b_p + PW * w + 9)
                    e.activation(sqb[0], sil[2], AF.Square)
                    ins = e.activation(sqb[1], sil[3], AF.Square)
                    FIN(e, svA, cA, ins)
                    e.wait_ge(cP.h, b_p + PW * w + 8)
                    e.activation(sil[4], pb[4][:, 0:512], AF.Silu)
                    ins = e.activation(sil[5], pb[5][:, 0:512], AF.Silu)
                    FIN(e, svA, cA, ins)
                    e.wait_ge(cP.h, b_p + PW * w + 9)
                    ins = e.activation(rrow[0:1, :], pb[7][0:1, 0:512], AF.Sqrt,
                                       scale=fc_sb[0:1, 3:4],
                                       bias=fc_sb[0:1, 1:2])
                    FIN(e, svA, cA, ins)
                    e.wait_ge(cP.h, b_p + PW * w + 11)
                    e.wait_ge(cV.h, b_v + VW * w + 7)
                    ins = e.activation(rrow[0:1, :], pb[7][0:1, 0:512], AF.Sqrt,
                                       scale=fc_sb[0:1, 5:6],
                                       bias=fc_sb[0:1, 6:7])
                    FIN(e, svA, cA, ins)
                    e.dma_start(beta_d[0:1, ts(w, 512)], betw[0:1, :]
                                ).then_inc(st_qkv.h, 16)
                    e.dma_start(v_s[0:128, ts(w, 512)], sil[4]
                                ).then_inc(st_qkv.h, 16)
                    e.dma_start(v_s[128:256, ts(w, 512)], sil[5]
                                ).then_inc(st_qkv.h, 16)
                    e.wait_ge(cV.h, b_v + VW * w + 8)
                    e.dma_start(q_s[0:128, ts(w, 512)], qn_sb[0]
                                ).then_inc(st_qkv.h, 16)
                    e.dma_start(q_s[128:256, ts(w, 512)], qn_sb[1]
                                ).then_inc(st_qkv.h, 16)
                    e.wait_ge(cV.h, b_v + VW * w + 10)
                    e.dma_start(k_s[0:128, ts(w, 512)], kn_sb[0]
                                ).then_inc(st_qkv.h, 16)
                    e.dma_start(k_s[128:256, ts(w, 512)], kn_sb[1]
                                ).then_inc(st_qkv.h, 16)
                    st_qkv.v += 112

    emit_proj()

    if MK_PH == "p3":
        with nc.Block() as blk:
            w_qkv = st_qkv.v
            sStub3 = Ctr(nc, "sStub3")
            cps = os.environ.get("MK_CP", "qkv")

            @blk.gpsimd
            def _(e):
                e.wait_ge(st_qkv.h, w_qkv)
                vw = out_d.bitcast(BF16).rearrange("(a b) c -> a (b c)", b=2)
                if cps == "qkv2":
                    pairs = [(vw[0:256, :], v_s),
                             (out_d.rearrange("(a b) c -> a (b c)", b=4)[128:129, :],
                              beta_d)]
                else:
                    pairs = [(vw[0:256, :], q_s), (vw[256:512, :], k_s)]
                for dst, srcp in pairs:
                    e.dma_start(dst, srcp).then_inc(sStub3.h, 16)
                    sStub3.v += 16
                e.wait_ge(sStub3.h, sStub3.v)

            @blk.sync
            def _(e):
                pass

            @blk.scalar
            def _(e):
                pass

            @blk.vector
            def _(e):
                pass

            @blk.tensor
            def _(e):
                pass
        return nc

    def emit_delta():
        ss_q, ss_k, ss_kt, ss_vt = 0, 1024, 2048, 3072
        NGE = int(os.environ.get("MK_NG", NG))

        def prep_ops(g):
            ops = [('T', ('bT', g)), ('V', ('bp', g)), ('V', ('nbv', g))]
            for l in range(4):
                ops += [('T', ('G0', l, g)), ('V', ('A', l, g))]
            for l in range(4):
                ops += [('T', ('AT', l, g)), ('V', ('init', l, g))]
            for l in range(4):
                ops += [('T', ('W2', l, g)), ('V', ('eLt', l, g))]
            for it in range(6):
                ops += [('T', ('P1', it, g)), ('V', ('eP2', it, g)),
                        ('T', ('X', it, g)), ('A', ('eX', it, g)),
                        ('T', ('XT', it, g)), ('A', ('eXT', it, g))]
            for l in range(4):
                ops += [('T', ('nG', l, g)), ('A', ('enG', l, g))]
            return ops

        def chain_ops(g):
            ops = []
            for l in range(4):
                ops += [('T', ('U', l, g)), ('A', ('eU', l, g)),
                        ('T', ('O', l, g)), ('V', ('eof', l, g)),
                        ('A', ('rms', l, g)), ('V', ('eon', l, g)),
                        ('A', ('st', l, g)),
                        ('T', ('Su', l, g)), ('V', ('eS', l, g))]
            return ops

        weave = [('V', ('S0',))]
        for g in range(NGE):
            prep = prep_ops(g)
            chain = chain_ops(g - 1) if g >= 1 else []
            ci = 0
            for idx, p in enumerate(prep):
                weave.append(p)
                while ci < len(chain) and \
                        (idx + 1) * len(chain) // len(prep) > ci:
                    weave.append(chain[ci])
                    ci += 1
            weave += chain[ci:]
        weave += chain_ops(NGE - 1)

        with nc.Block() as blk:
            ar.reset()
            Xb2 = [ar.take([128, 512], BF16) for _ in range(2)]
            Xtb = ar.take([128, 512], BF16)
            P2b = ar.take([128, 512], BF16)
            A_sb = ar.take([128, 512], BF16)
            AT_sb = ar.take([128, 512], BF16)
            Lt2 = [ar.take([128, 512], BF16) for _ in range(2)]
            negG2 = [ar.take([128, 1024], BF16) for _ in range(2)]
            nbk2 = [ar.take([128, 1024], BF16) for _ in range(2)]
            bV2 = [ar.take([128, 1024], BF16) for _ in range(2)]
            U_sb = ar.take([128, 1024], BF16)
            S_sb = ar.take([128, 1024], BF16)
            S_f32 = ar.take([128, 1024], F32)
            o_n = ar.take([128, 1024], BF16)
            o_f = ar.take([128, 256], F32)
            scr2 = ar.take([128, 256], F32)
            osum = ar.take([128, 4], F32)
            osr = ar.take([128, 4], F32)
            bp = ar.take([128, 4], F32)
            bet_row = ar.take([1, 1024], F32)

            b_sB = [s.v for s in sB]
            st_ol = [Ctr(nc, f"st_ol{i}") for i in range(4)]
            w_qkv = st_qkv.v

            pb7b = pb[7].bitcast(BF16)
            pb3b = pb[3].bitcast(BF16)
            pb4b = pb[4].bitcast(BF16)

            posP, posV, posA, posS = {}, {}, {}, {}
            pP, pV, pA = cP.v, cV.v, cA.v
            pS = [c.v for c in st_ol]
            for eng, key in weave:
                if eng == 'T':
                    pP += 1
                    posP[key] = pP
                elif eng == 'V':
                    pV += 1
                    posV[key] = pV
                elif key[0] in ('rms', 'eX', 'eXT', 'eU', 'enG'):
                    pA += 2 if key[0] == 'rms' else 1
                    posA[key] = pA
                else:
                    l, g = key[1], key[2]
                    pS[l] += 16
                    posS[(l, g)] = (st_ol[l].h, pS[l])

            def lane_bi(l):
                return l % 2, (1 if l >= 2 else 0)

            def kq_tile(buf, l, h):
                return buf[:, 512 * (l % 2) + 256 * h + 128 * (l // 2):
                           512 * (l % 2) + 256 * h + 128 * (l // 2) + 128]

            @blk.sync
            def _(e):
                e.wait_ge(st_qkv.h, w_qkv)
                for g in range(NGE):
                    s = g % 2
                    if g >= 2:
                        e.wait_ge(cP.h, posP[('Su', 3, g - 2)])
                        e.wait_ge(cV.h, posV[('eS', 3, g - 2)])
                    for b in range(2):
                        for h in range(2):
                            e.dma_start(
                                tarena[:, s * 8192 + ss_q + 512 * b + 256 * h:
                                       s * 8192 + ss_q + 512 * b + 256 * h + 256],
                                q_s[128 * h:128 * (h + 1),
                                    4096 * b + 256 * g:4096 * b + 256 * (g + 1)]
                            ).then_inc(sB[s].h, 16)
                            e.dma_start(
                                tarena[:, s * 8192 + ss_k + 512 * b + 256 * h:
                                       s * 8192 + ss_k + 512 * b + 256 * h + 256],
                                k_s[128 * h:128 * (h + 1),
                                    4096 * b + 256 * g:4096 * b + 256 * (g + 1)]
                            ).then_inc(sB[s].h, 16)
                    for l in range(4):
                        b, i = lane_bi(l)
                        cc = 2 * g + i
                        e.dma_start(
                            tarena[:, s * 8192 + ss_kt + 256 * l:
                                   s * 8192 + ss_kt + 256 * (l + 1)],
                            k_s[:, 4096 * b + 128 * cc:4096 * b + 128 * (cc + 1)],
                            transpose=True).then_inc(sB[s].h, 16)
                        e.dma_start(
                            tarena[:, s * 8192 + ss_vt + 256 * l:
                                   s * 8192 + ss_vt + 256 * (l + 1)],
                            v_s[:, 4096 * b + 128 * cc:4096 * b + 128 * (cc + 1)],
                            transpose=True).then_inc(sB[s].h, 16)
                    for b in range(2):
                        e.dma_start(
                            bet_row[0:1, 512 * s + 256 * b:512 * s + 256 * (b + 1)],
                            beta_d[0:1, 4096 * b + 256 * g:4096 * b + 256 * (g + 1)]
                        ).then_inc(sB[s].h, 16)
                    sB[s].v += 18 * 16

            @blk.tensor
            def _(e):
                for eng, key in weave:
                    if eng != 'T':
                        continue
                    nm, g = key[0], key[-1]
                    s = g % 2
                    kch = tarena[:, s * 8192 + ss_k:s * 8192 + ss_k + 1024]
                    qch = tarena[:, s * 8192 + ss_q:s * 8192 + ss_q + 1024]
                    if nm == 'bT':
                        e.wait_ge(sB[s].h, b_sB[s] + 288 * (g // 2 + 1))
                        if g >= 1:
                            e.wait_ge(cV.h, posV[('init', 3, g - 1)])
                        for l in range(4):
                            b, i = lane_bi(l)
                            ins = e.transpose(
                                pb[7][:, 256 + l:256 + l + 1],
                                bet_row[0:1, 512 * s + 256 * b + 128 * i:
                                        512 * s + 256 * b + 128 * (i + 1)],
                                identf_sb[0:1, 0:1])
                    elif nm == 'G0':
                        l = key[1]
                        if l == 0 and g >= 1:
                            e.wait_ge(cA.h, posA[('enG', 3, g - 1)])
                        if l >= 2:
                            e.wait_ge(cV.h, posV[('A', l - 2, g)])
                        for h in range(2):
                            ins = e.matmul(pb[2 + l % 2][:, 0:128],
                                           kq_tile(kch, l, h),
                                           kq_tile(kch, l, h),
                                           start=(h == 0), stop=(h == 1))
                    elif nm == 'AT':
                        l = key[1]
                        e.wait_ge(cV.h, posV[('A', l, g)])
                        if l < 2:
                            e.wait_ge(cV.h, posV[('bp', g)])
                        else:
                            e.wait_ge(cV.h, posV[('init', l - 2, g)])
                        dst = pb7b if l % 2 == 0 else pb4b
                        ins = e.transpose(dst[:, 0:128],
                                          A_sb[:, ts(l, 128)], ident)
                    elif nm == 'W2':
                        l = key[1]
                        e.wait_ge(cV.h, posV[('A', 2 + l % 2, g)])
                        if l >= 2:
                            e.wait_ge(cV.h, posV[('eLt', l - 2, g)])
                        for h in range(2):
                            ins = e.matmul(pb[2 + l % 2][:, 0:128],
                                           kq_tile(kch, l, h),
                                           kq_tile(qch, l, h),
                                           start=(h == 0), stop=(h == 1))
                    elif nm == 'P1':
                        it = key[1]
                        if it == 0:
                            e.wait_ge(cV.h, posV[('init', 3, g)])
                            e.wait_ge(cV.h, posV[('eLt', 3, g)])
                        else:
                            e.wait_ge(cA.h, posA[('eXT', it - 1, g)])
                        for l in range(4):
                            e.matmul(pb[3][:, ts(l, 128)], ident,
                                     Xb2[s][:, ts(l, 128)],
                                     start=True, stop=False)
                            ins = e.matmul(pb[3][:, ts(l, 128)],
                                           A_sb[:, ts(l, 128)],
                                           Xb2[s][:, ts(l, 128)],
                                           start=False, stop=True)
                    elif nm == 'X':
                        it = key[1]
                        e.wait_ge(cV.h, posV[('eP2', it, g)])
                        for l in range(4):
                            ins = e.matmul(pb[4][:, ts(l, 128)],
                                           Xtb[:, ts(l, 128)],
                                           P2b[:, ts(l, 128)],
                                           start=True, stop=True)
                    elif nm == 'XT':
                        it = key[1]
                        e.wait_ge(cA.h, posA[('eX', it, g)])
                        for l in range(4):
                            ins = e.transpose(pb3b[:, ts(l, 128)],
                                              Xb2[s][:, ts(l, 128)], ident)
                    elif nm == 'nG':
                        l = key[1]
                        if l == 0:
                            e.wait_ge(cV.h, posV[('eLt', 2, g)])
                            e.wait_ge(cA.h, posA[('eX', 5, g)])
                        elif l == 1:
                            e.wait_ge(cV.h, posV[('eLt', 3, g)])
                            e.wait_ge(cA.h, posA[('eXT', 5, g)])
                        else:
                            e.wait_ge(cA.h, posA[('enG', l - 2, g)])
                        for h in range(2):
                            ins = e.matmul(pb[2 + l % 2][:, 128 * h:
                                                         128 * (h + 1)],
                                           nbk2[s][:, 256 * l + 128 * h:
                                                   256 * l + 128 * (h + 1)],
                                           Xb2[s][:, ts(l, 128)],
                                           start=True, stop=True)
                    elif nm == 'U':
                        l = key[1]
                        sq = l % 2
                        e.wait_ge(cA.h, posA[('enG', l, g)])
                        if l >= 2:
                            e.wait_ge(cV.h, posV[('eS', l - 2, g)])
                        elif g >= 1:
                            e.wait_ge(cV.h, posV[('eS', l + 2, g - 1)])
                            if l == 0:
                                e.wait_ge(cA.h, posA[('eU', 3, g - 1)])
                        else:
                            e.wait_ge(cV.h, posV[('S0',)])
                        e.matmul(pb[5][:, 0:256], Xb2[s][:, ts(l, 128)],
                                 bV2[s][:, ts(l, 256)], start=True, stop=False)
                        for h in range(2):
                            ins = e.matmul(pb[5][:, 0:256],
                                           negG2[s][:, 256 * l + 128 * h:
                                                    256 * l + 128 * (h + 1)],
                                           S_sb[:, 512 * sq + 256 * h:
                                                512 * sq + 256 * (h + 1)],
                                           start=False, stop=(h == 1))
                    elif nm == 'O':
                        l = key[1]
                        sq = l % 2
                        e.wait_ge(cA.h, posA[('eU', l, g)])
                        if l == 0 and g >= 1:
                            e.wait_ge(cV.h, posV[('eof', 3, g - 1)])
                        for h in range(2):
                            e.matmul(pb[6][:, 0:256], kq_tile(qch, l, h),
                                     S_sb[:, 512 * sq + 256 * h:
                                          512 * sq + 256 * (h + 1)],
                                     start=(h == 0), stop=False)
                        ins = e.matmul(pb[6][:, 0:256], Lt2[s][:, ts(l, 128)],
                                       U_sb[:, ts(l, 256)],
                                       start=False, stop=True)
                    elif nm == 'Su':
                        l = key[1]
                        sq = l % 2
                        for h in range(2):
                            ins = e.matmul(
                                pb[sq][:, ts(h, 256)],
                                tarena[:, s * 8192 + ss_kt + 256 * l + 128 * h:
                                       s * 8192 + ss_kt + 256 * l + 128 * (h + 1)],
                                U_sb[:, ts(l, 256)],
                                start=(h == 0), stop=(h == 1))
                    ins.then_inc(cP.h, 1)
                    cP.v += 1
                    assert cP.v == posP[key], (key, cP.v, posP[key])

            @blk.vector
            def _(e):
                for eng, key in weave:
                    if eng != 'V':
                        continue
                    nm = key[0]
                    if nm == 'S0':
                        e.wait_ge(st_qkv.h, w_qkv)
                        e.memset(S_sb[:, :], 0)
                        ins = e.memset(S_f32[:, :], 0)
                        FIN(e, svV, cV, ins)
                        assert cV.v == posV[key]
                        continue
                    g = key[-1]
                    s = g % 2
                    ktok = tarena[:, s * 8192 + ss_kt:s * 8192 + ss_kt + 1024]
                    vtok = tarena[:, s * 8192 + ss_vt:s * 8192 + ss_vt + 1024]
                    if nm == 'bp':
                        e.wait_ge(cP.h, posP[('bT', g)])
                        ins = e.tensor_copy(bp[:, 0:4], pb[7][:, 256:260])
                        FIN(e, svV, cV, ins)
                    elif nm == 'nbv':
                        for l in range(4):
                            e.tensor_scalar(nbk2[s][:, ts(l, 256)],
                                            ktok[:, ts(l, 256)],
                                            bp[:, l:l + 1], -1.0,
                                            ALU.mult, ALU.mult)
                            ins = e.tensor_scalar(bV2[s][:, ts(l, 256)],
                                                  vtok[:, ts(l, 256)],
                                                  bp[:, l:l + 1], None,
                                                  ALU.mult)
                        FND(e, cV, ins)
                    elif nm == 'A':
                        l = key[1]
                        e.wait_ge(cP.h, posP[('G0', l, g)])
                        ins = e.scalar_tensor_tensor(A_sb[:, ts(l, 128)],
                                                     pb[2 + l % 2][:, 0:128],
                                                     bp[:, l:l + 1], slm,
                                                     ALU.mult, ALU.mult)
                        FND(e, cV, ins)
                    elif nm == 'init':
                        l = key[1]
                        e.wait_ge(cP.h, posP[('AT', l, g)])
                        ins = e.tensor_copy(AT_sb[:, ts(l, 128)],
                                            (pb7b if l % 2 == 0 else
                                             pb4b)[:, 0:128])
                        SY(e, svV, ins)
                        e.tensor_tensor(Xb2[s][:, ts(l, 128)], ident,
                                        AT_sb[:, ts(l, 128)], ALU.subtract)
                        ins = e.tensor_tensor(Xtb[:, ts(l, 128)], ident,
                                              A_sb[:, ts(l, 128)],
                                              ALU.subtract)
                        FND(e, cV, ins)
                    elif nm == 'eP2':
                        e.wait_ge(cP.h, posP[('P1', key[1], g)])
                        ins = e.tensor_tensor(P2b[:, :], twoI, pb[3][:, 0:512],
                                              ALU.subtract)
                        FND(e, cV, ins)
                    elif nm == 'eLt':
                        l = key[1]
                        e.wait_ge(cP.h, posP[('W2', l, g)])
                        ins = e.tensor_tensor(Lt2[s][:, ts(l, 128)],
                                              pb[2 + l % 2][:, 0:128], ium,
                                              ALU.mult)
                        FND(e, cV, ins)
                    elif nm == 'eof':
                        l = key[1]
                        e.wait_ge(cP.h, posP[('O', l, g)])
                        ins = e.tensor_copy(o_f[:, :], pb[6][:, 0:256])
                        FND(e, cV, ins)
                    elif nm == 'eon':
                        l = key[1]
                        e.wait_ge(cA.h, posA[('rms', l, g)])
                        if g >= 1:
                            e.wait_ge(*posS[(l, g - 1)])
                        ins = e.reciprocal(osr[:, 1:2], osr[:, 0:1])
                        SY(e, svV, ins)
                        ins = e.tensor_scalar(o_n[:, ts(l, 256)], o_f[:, :],
                                              osr[:, 1:2], None, ALU.mult)
                        FND(e, cV, ins)
                    elif nm == 'eS':
                        l = key[1]
                        sq = l % 2
                        e.wait_ge(cP.h, posP[('Su', l, g)])
                        ins = e.tensor_tensor(S_f32[:, ts(sq, 512)],
                                              S_f32[:, ts(sq, 512)],
                                              pb[sq][:, 0:512], ALU.add)
                        SY(e, svV, ins)
                        ins = e.tensor_copy(S_sb[:, ts(sq, 512)],
                                            S_f32[:, ts(sq, 512)])
                        FND(e, cV, ins)
                    assert cV.v == posV[key], (key, cV.v, posV[key])

            @blk.scalar
            def _(e):
                for eng, key in weave:
                    if eng != 'A':
                        continue
                    nm, l, g = key
                    s = g % 2
                    if nm == 'eX':
                        e.wait_ge(cP.h, posP[('X', l, g)])
                        ins = e.activation(Xb2[s][:, :], pb[4][:, 0:512],
                                           AF.Copy)
                        FND(e, cA, ins)
                        assert cA.v == posA[key]
                        continue
                    if nm == 'eXT':
                        e.wait_ge(cP.h, posP[('XT', l, g)])
                        ins = e.activation(Xtb[:, :], pb3b[:, 0:512], AF.Copy)
                        FND(e, cA, ins)
                        assert cA.v == posA[key]
                        continue
                    if nm == 'eU':
                        e.wait_ge(cP.h, posP[('U', l, g)])
                        ins = e.activation(U_sb[:, ts(l, 256)],
                                           pb[5][:, 0:256], AF.Copy)
                        FND(e, cA, ins)
                        assert cA.v == posA[key]
                        continue
                    if nm == 'enG':
                        e.wait_ge(cP.h, posP[('nG', l, g)])
                        ins = e.activation(negG2[s][:, ts(l, 256)],
                                           pb[2 + l % 2][:, 0:256], AF.Copy)
                        FND(e, cA, ins)
                        assert cA.v == posA[key]
                        continue
                    if nm == 'rms':
                        e.wait_ge(cV.h, posV[('eof', l, g)])
                        ins = e.activation(scr2[:, :], o_f[:, :], AF.Square,
                                           accum_out=osum[:, 0:1])
                        SY(e, svA, ins)
                        ins = e.activation(osr[:, 0:1], osum[:, 0:1], AF.Sqrt,
                                           scale=fc_sb[:, 4:5],
                                           bias=fc_sb[:, 2:3])
                        FIN(e, svA, cA, ins, 2)
                        assert cA.v == posA[key], (key, cA.v, posA[key])
                    else:
                        b, i = lane_bi(l)
                        cc = 2 * g + i
                        e.wait_ge(cV.h, posV[('eon', l, g)])
                        e.dma_start(
                            o_c[4096 * b + 128 * cc:4096 * b + 128 * (cc + 1), :],
                            o_n[:, ts(l, 256)]).then_inc(st_ol[l].h, 16)
                        st_ol[l].v += 16

            @blk.gpsimd
            def _(e):
                if NGE == NG:
                    for c_ in st_ol:
                        e.wait_ge(c_.h, posS[(st_ol.index(c_), 7)][1])
                    for r0 in (0, 4096):
                        e.collective_compute(
                            "AllGather", ALU.bypass, rg,
                            [o_c[r0:r0 + 2048, :].opt()],
                            [go[(r0 // 4096) * 16384:
                                (r0 // 4096) * 16384 + 16384, :].opt()]
                        ).then_inc(sC.h)
                        sC.v += 1
                    for c_ in st_ol:
                        e.wait_ge(c_.h, c_.v)
                    for r0 in (2048, 6144):
                        e.collective_compute(
                            "AllGather", ALU.bypass, rg,
                            [o_c[r0:r0 + 2048, :].opt()],
                            [go[32768 + (r0 // 4096) * 16384:
                                32768 + (r0 // 4096) * 16384 + 16384, :].opt()]
                        ).then_inc(sC.h)
                        sC.v += 1
        return st_ol

    st_ol_g = emit_delta()

    if MK_PH == "p4":
        with nc.Block() as blk:
            w_ol = [(c.h, c.v) for c in st_ol_g]
            sStub4 = Ctr(nc, "sStub4")

            @blk.gpsimd
            def _(e):
                for hh_, vv_ in w_ol:
                    e.wait_ge(hh_, vv_)
                e.dma_start(out_d.bitcast(BF16)[:, 0:D],
                            o_c.rearrange("(a b) c -> a (b c)", b=8)
                            ).then_inc(sStub4.h, 16)
                sStub4.v += 16
                e.wait_ge(sStub4.h, sStub4.v)

            @blk.sync
            def _(e):
                pass

            @blk.scalar
            def _(e):
                pass

            @blk.vector
            def _(e):
                pass

            @blk.tensor
            def _(e):
                pass
        return nc

    def emit_outproj():
        with nc.Block() as blk:
            ar.reset()
            wof = [ar.take([128, D], F32) for _ in range(2)]
            x1sl = [ar.take([128, D], F32) for _ in range(2)]
            x2sl = [ar.take([128, D], F32) for _ in range(2)]
            sGo = Ctr(nc, "sGo")
            stx2 = [Ctr(nc, "stx2_0"), Ctr(nc, "stx2_1")]
            wo_bf = hid[:, 0:NDC * D]
            b_wl = [w.v for w in wl]
            b_xl = [w.v for w in xl]
            b_p, b_v, b_a = cP.v, cV.v, cA.v
            w_coll = sC.v
            x1w = [(c.h, c.v) for c in x1_st]

            @blk.sync
            def _(e):
                e.wait_ge(sC.h, w_coll)
                for h in range(H):
                    for e2 in range(2):
                        k = 2 * h + e2
                        sv = e.value_load(ooff_sb[k:k + 1, 0:1])
                        blk_ap = dataclasses.replace(go[0:NT, 0:128],
                                                     offset=sv)
                        e.dma_start(tarena[:, ts(k, 1024)], blk_ap,
                                    transpose=True).then_inc(sGo.h, 16)
                        sGo.v += 16

            @blk.scalar
            def _(e):
                for k in range(2):
                    e.dma_start(wof[k], wo_in[ts(k, 128), :]).then_inc(
                        wl[k].h, 16)
                    wl[k].v += 16
                for k in range(NDC):
                    e.wait_ge(wl[k % 2].h, b_wl[k % 2] + 16 * (k // 2 + 1))
                    ins = e.activation(wo_bf[:, ts(k, D)], wof[k % 2], AF.Copy,
                                       scale=nrmw_sb[:, k:k + 1])
                    FIN(e, svA, cA, ins)
                    if k + 2 < NDC:
                        e.dma_start(wof[k % 2], wo_in[ts(k + 2, 128), :]
                                    ).then_inc(wl[k % 2].h, 16)
                        wl[k % 2].v += 16
                for hh_, vv_ in x1w:
                    e.wait_ge(hh_, vv_)
                for t in range(8):
                    if t >= 2:
                        e.wait_ge(cV.h, b_v + 4 * (t - 1))
                    e.dma_start(x1sl[t % 2], x1b[ts(t, 128), :]).then_inc(
                        xl[t % 2].h, 16)
                    xl[t % 2].v += 16

            @blk.tensor
            def _(e):
                e.wait_ge(sGo.h, sGo.v)
                for t in range(8):
                    if t >= 2:
                        e.wait_ge(cV.h, b_v + 4 * (t - 1))
                    pbs = 0 if t % 2 == 0 else 4
                    for k in range(NDC):
                        if t == 0:
                            e.wait_ge(cA.h, b_a + k + 1)
                        for dq in range(4):
                            ins = e.matmul(
                                pb[pbs + dq][:, 0:512],
                                tarena[:, 1024 * k + 128 * t:
                                       1024 * k + 128 * (t + 1)],
                                wo_bf[:, D * k + 512 * dq:D * k + 512 * (dq + 1)],
                                start=(k == 0), stop=(k == NDC - 1))
                    ins.then_inc(cP.h, 1)
                    cP.v += 1

            @blk.vector
            def _(e):
                for t in range(8):
                    e.wait_ge(cP.h, b_p + t + 1)
                    e.wait_ge(xl[t % 2].h, b_xl[t % 2] + 16 * (t // 2 + 1))
                    if t >= 2:
                        e.wait_ge(stx2[t % 2].h, 16 * (t // 2))
                    pbs = 0 if t % 2 == 0 else 4
                    for dq in range(4):
                        ins = e.tensor_tensor(x2sl[t % 2][:, ts(dq, 512)],
                                              pb[pbs + dq][:, 0:512],
                                              x1sl[t % 2][:, ts(dq, 512)],
                                              ALU.add)
                        FND(e, cV, ins)

            @blk.gpsimd
            def _(e):
                for t in range(8):
                    e.wait_ge(cV.h, b_v + 4 * t + 4)
                    e.dma_start(x2b[ts(t, 128), :],
                                x2sl[t % 2]).then_inc(stx2[t % 2].h, 16)
                    stx2[t % 2].v += 16
        return stx2

    x2_st = emit_outproj()
    emit_ln(x2b, y2b, x2_st, st_y2)

    emit_gateup(y2b, st_y2, use_int=False, cache_bf=False)
    emit_down(x2b, out_d, x2_st, use_int=True, cache_bf=False,
              final=True, tag='p6')

    if MK_DBG:
        src_map = {"x1": x1b, "y1": None, "x2": x2b}
        dsrc = src_map.get(MK_DBG, x1b)
        with nc.Block() as blk:
            @blk.gpsimd
            def _(e):
                e.dma_start(dbg_d, dsrc).then_inc(sC.h, 16)
                sC.v += 16
                e.wait_ge(sC.h, sC.v)

            @blk.sync
            def _(e):
                pass

            @blk.scalar
            def _(e):
                pass

            @blk.vector
            def _(e):
                pass

            @blk.tensor
            def _(e):
                pass

    return nc


_NC_CACHE = {}


def _host_layout(inputs):
    x = np.asarray(inputs["L0_attn_hidden_states"])
    wq = np.asarray(inputs["L0_attn_Wq"])
    wk = np.asarray(inputs["L0_attn_Wk"])
    wv = np.asarray(inputs["L0_attn_Wv"])
    wb = np.asarray(inputs["L0_attn_Wb"])
    cq = np.asarray(inputs["L0_attn_conv_q"])
    ck = np.asarray(inputs["L0_attn_conv_k"])
    cv = np.asarray(inputs["L0_attn_conv_v"])
    nw = np.asarray(inputs["L0_attn_norm_w"])
    wo = np.asarray(inputs["L0_attn_Wo"])
    wqg = np.asarray(inputs["L0_ffn_wq_gate"])
    sg = np.asarray(inputs["L0_ffn_s_gate"])
    wqu = np.asarray(inputs["L0_ffn_wq_up"])
    su = np.asarray(inputs["L0_ffn_s_up"])
    wqd = np.asarray(inputs["L0_ffn_wq_down"])
    sd = np.asarray(inputs["L0_ffn_s_down"])

    bf = ml_dtypes.bfloat16
    xf = np.ascontiguousarray(x.reshape(GT, D)).astype(np.float32)
    wg_t = np.ascontiguousarray(
        wqg.reshape(NHC, 128, NDC, 128).transpose(0, 3, 2, 1).reshape(NHC, 128, D)
    ).astype(np.int32)
    wu_t = np.ascontiguousarray(
        wqu.reshape(NHC, 128, NDC, 128).transpose(0, 3, 2, 1).reshape(NHC, 128, D)
    ).astype(np.int32)
    wd_t = np.ascontiguousarray(
        wqd.reshape(4, 512, NHC, 128).transpose(0, 2, 3, 1)
    ).astype(np.int32)
    sg_in = np.ascontiguousarray(sg.reshape(NHC, 128).T).astype(np.float32)
    su_in = np.ascontiguousarray(su.reshape(NHC, 128).T).astype(np.float32)
    sd_in = np.ascontiguousarray(sd.reshape(1, D)).astype(np.float32)
    nrmw_in = np.ascontiguousarray(
        np.tile(nw, NCORE).reshape(NDC, 128).T).astype(np.float32)
    wo_in = np.ascontiguousarray(wo).astype(np.float32)

    ident = np.eye(128, dtype=bf)
    identf = np.eye(128, dtype=np.float32)
    twoI = np.ascontiguousarray(
        np.tile(2.0 * np.eye(128, dtype=np.float32), (1, 4))).astype(bf)
    ii, jj = np.meshgrid(np.arange(128), np.arange(128), indexing="ij")
    slmask = (ii > jj).astype(bf)
    iumask = (ii <= jj).astype(bf)
    ones1 = np.ones((1, 128), dtype=bf)
    half1 = np.full((1, 128), 0.5, dtype=bf)

    in_maps = []
    for j in range(NCORE):
        wqkv = np.concatenate(
            [wq[:, ts(j, 256)], wk[:, ts(j, 256)], wv[:, ts(j, 256)]], axis=1)
        wqkv_j = np.ascontiguousarray(
            wqkv.reshape(NDC, 128, 768)).astype(np.float32)
        wb_j = np.ascontiguousarray(
            wb[:, j].reshape(NDC, 128, 1)).astype(np.float32)
        convw_j = np.stack([
            np.concatenate([cq[ts(j, 256)][ts(c, 128)],
                            ck[ts(j, 256)][ts(c, 128)],
                            cv[ts(j, 256)][ts(c, 128)]], axis=1)
            for c in range(2)]).astype(np.float32)
        in_maps.append({
            "x_loc": np.ascontiguousarray(xf[ts(j, NT)]),
            "wg_t": wg_t, "wu_t": wu_t, "wd_t": wd_t,
            "sg_in": sg_in, "su_in": su_in, "sd_in": sd_in,
            "wqkv_in": wqkv_j, "wb_in": wb_j, "convw_in": convw_j,
            "wo_in": wo_in, "nrmw_in": nrmw_in,
            "c_ident": ident, "c_identf": identf, "c_twoI": twoI,
            "c_slmask": slmask, "c_iumask": iumask,
            "c_ones1": ones1, "c_half1": half1,
            "ooff_in": np.array(
                [[(16384 * [0, 0, 2, 2, 1, 1, 3, 3][j] + 2048 * h
                   + 1024 * (j % 2)) * DV + 128 * e2]
                 for h in range(H) for e2 in range(2)],
                dtype=np.int32),
            "c_onesc": np.ones((128, 1), dtype=bf),
            "c_ones1f": np.ones((1, 128), dtype=np.float32),
            "fconst_in": np.tile(np.array(
                [[LN_EPS, 256 * L2_EPS, RMS_EPS, 256.0, 1.0 / 256, 1.0, L2_EPS,
                  0.0]], dtype=np.float32), (128, 1)),
        })
    return in_maps


def kernel(**inputs):
    in_maps = _host_layout(inputs)
    if "nc" not in _NC_CACHE:
        _NC_CACHE["nc"] = build_nc()
    nc = _NC_CACHE["nc"]
    res = run_bass_kernel_spmd(nc, in_maps, core_ids=list(range(NCORE)))
    out = np.concatenate([res.results[j]["out_loc"] for j in range(NCORE)],
                         axis=0)
    return out.reshape(B, T, D)
